# version 02
# baseline (speedup 1.0000x reference)
import sys
sys.path.insert(0, '/opt/trn_rl_repo')
from contextlib import ExitStack
from dataclasses import dataclass

import numpy as np
import ml_dtypes

import concourse.mybir as mybir
from concourse.ap import AP
from concourse.bacc import Bacc
from concourse.bass_core_rust import IndirectOffsetOnAxis

F32 = mybir.dt.float32
BF16 = mybir.dt.bfloat16
I32 = mybir.dt.int32
I16 = mybir.dt.int16
AF = mybir.ActivationFunctionType
ALU = mybir.AluOpType
AX = mybir.AxisListType
BF = ml_dtypes.bfloat16


@dataclass(frozen=True)
class Cfg:
    S: int = 2048
    D: int = 1024
    H: int = 16
    dh: int = 64
    F: int = 4096
    NC: int = 8
    E: int = 16

    @property
    def KC(self): return self.D // 128
    @property
    def TC(self): return self.S // 128
    @property
    def EPC(self): return self.E // self.NC
    @property
    def C(self): return self.NC * self.S // self.E
    @property
    def NS(self): return self.EPC * self.C
    @property
    def NJ(self): return self.NC * self.TC
    @property
    def QCW(self): return min(512, self.S)
    @property
    def NQC(self): return self.S // self.QCW
    @property
    def FC(self): return self.F // 128
    @property
    def TALL(self): return self.NC * self.S
    @property
    def OC(self): return (self.H * self.dh) // 128
    @property
    def NSC(self): return self.NS // 128
    @property
    def GW(self): return 2 * self.S - 128


FULL = Cfg()
MINI = Cfg(S=256, D=256, H=4, F=512)
MIDI = Cfg(S=1024, D=256, H=4, F=512)
MIDI2 = Cfg(S=1024, D=1024, H=16, F=512)
MIDI3 = Cfg(S=2048, D=256, H=4, F=512)


class Sem:
    def __init__(self, nc, es, name):
        self.h = es.enter_context(nc.semaphore(name))
        self.n = 0


class Rec:
    ENGINES = ("sync", "gpsimd", "scalar", "vector", "tensor")

    def __init__(self):
        self.ops = {e: [] for e in self.ENGINES}

    def emit(self, eng, make, inc=None, inck=1, waits=()):
        ws = []
        for w in waits:
            if isinstance(w, Sem):
                ws.append((w.h, w.n))
            else:
                s, v = w
                ws.append((s.h, v))
        tgt = None
        if inc is not None:
            inc.n += inck
            tgt = inc.h

        def fn(e, ws=ws, make=make, tgt=tgt, k=inck):
            for h, v in ws:
                if v > 0:
                    e.wait_ge(h, v)
            inst = make(e)
            if tgt is not None and inst is not None:
                inst.then_inc(tgt, k)
        self.ops[eng].append(fn)

    def replay(self, block):
        for name in self.ENGINES:
            fns = self.ops[name]

            def run(eng, fns=fns):
                for f in fns:
                    f(eng)
            getattr(block, name)(run)


def build(cfg: Cfg, debug=()):
    S, D, H, dh, F_, NC, E = cfg.S, cfg.D, cfg.H, cfg.dh, cfg.F, cfg.NC, cfg.E
    KC, TC, EPC, C, NS, NJ, QCW, NQC, FC, TALL, OC, NSC, GW = (
        cfg.KC, cfg.TC, cfg.EPC, cfg.C, cfg.NS, cfg.NJ, cfg.QCW, cfg.NQC,
        cfg.FC, cfg.TALL, cfg.OC, cfg.NSC, cfg.GW)
    SCALE = 1.0 / float(np.sqrt(dh))
    TROW = 2 * S - 1
    G5 = 4
    NW = NJ // 16
    NKP = TC // 2
    TGW = min(256, C)
    NTG = C // TGW
    NT2 = TGW // 128

    nc = Bacc("TRN2", target_bir_lowering=False)

    def inp(name, shape, dt=F32):
        return nc.dram_tensor(name, list(shape), dt, kind="ExternalInput").ap()

    x_in = inp("x", (S, D))
    wq_in = inp("wq", (128, OC, KC, 128))
    wk_in = inp("wk", (128, OC, KC, 128))
    wv_in = inp("wv", (128, KC, D))
    wo_in = inp("wo", (128, OC, D))
    wg_in = inp("wg", (128, KC, E))
    w1_in = [inp(f"w1_{e}", (128, FC, KC, 128)) for e in range(EPC)]
    w2_in = [inp(f"w2_{e}", (128, FC, D)) for e in range(EPC)]
    bp_in = inp("bp", (16, 1))
    ba_in = inp("ba", (16, 1))
    fr_in = inp("fr", (1, dh // 2))
    twoc_in = inp("twoc", (128, 1))
    sown_in = inp("sown", (128, TC))

    out_t = nc.dram_tensor("out", [S, D], F32, kind="ExternalOutput").ap()
    dbg = {}

    def dbg_out(name, shape, dt=F32):
        d = nc.dram_tensor("dbg_" + name, list(shape), dt, kind="ExternalOutput")
        dbg[name] = d.ap()
        return dbg[name]

    def const(name, arr):
        return nc.inline_tensor(np.ascontiguousarray(arr), name).ap()

    eye_f32c = const("eye_f32c", np.eye(128, dtype=np.float32))
    eye_bf16c = const("eye_bf16c", np.eye(128, dtype=np.float32).astype(BF))
    tril_ic = const("tril_ic", np.triu(np.ones((128, 128), np.float32)).astype(BF))
    tril_sc = const("tril_sc", np.triu(np.ones((128, 128), np.float32), 1))
    ones_rc = const("ones_rc", np.ones((1, 128), np.float32))
    ones_cc = const("ones_cc", np.ones((128, 1), np.float32).astype(BF))
    rev16c = const("rev16c", np.broadcast_to(
        (E - 1.0) - np.arange(E, dtype=np.float32), (128, E)).copy())
    posc = const("posc", np.arange(S, dtype=np.float32).reshape(1, S))
    _i = np.arange(TALL)
    _scol = _i // 16
    _qe = _i % 16
    _j = _scol % NJ
    _t = _scol // NJ
    _gv = 128 * _j + 16 * _t + _qe
    iopc = const("iopc", np.broadcast_to(
        np.stack([_gv // 128, _gv % 128], 1).reshape(1, -1).astype(np.float32),
        (16, 2 * TALL)).astype(BF).copy())

    def idram(name, shape, dt):
        return nc.dram_tensor(name, list(shape), dt, kind="Internal")

    xn_dram = idram("xn_dram", (S, D), BF16)
    ag_xn = idram("ag_xn", (TALL, D), BF16)
    lg_dram = idram("lg_dram", (E, S), F32)
    ag_lg = idram("ag_lg", (NC * E, S), F32)
    dT_dram = idram("dT_dram", (128, NJ), F32)
    tot_dram = idram("tot_dram", (NJ, E), F32)
    off_dram = idram("off_dram", (NJ, E), F32)
    tok_dram = idram("tok_dram", (NSC, 128), F32)
    tall_dram = idram("tall_dram", (H, TROW), BF16)
    y2_dram = idram("y2_dram", (NS, D), BF16)
    ag_y2 = idram("ag_y2", (NC * NS, D), BF16)

    es = ExitStack()

    def sb(name, shape, dt):
        return es.enter_context(nc.sbuf_tensor(name, list(shape), dt))

    class Region:
        def __init__(self, name, nbytes):
            self.t = sb(name, (128, nbytes // 2), BF16)
            self.off = 0

        def reset(self):
            self.off = 0

        def _take(self, nbytes):
            o = self.off
            self.off += ((nbytes + 31) // 32) * 32
            assert self.off <= self.t.shape[1] * 2, \
                f"{self.t.name} overflow {self.off}"
            return o

        def bf_at(self, off, *shape):
            saved = self.off
            self.off = off
            v = self.bf(*shape)
            self.off = max(saved, self.off)
            return v

        def bf(self, *shape):
            n = int(np.prod(shape))
            o = self._take(2 * n)
            v = self.t[:, o // 2:o // 2 + n]
            return self._shape(v, shape)

        def f32(self, *shape):
            n = int(np.prod(shape))
            o = self._take(4 * n)
            v = self.t[:, o // 2:o // 2 + 2 * n].bitcast(F32)
            return self._shape(v, shape)

        @staticmethod
        def _shape(v, shape):
            if len(shape) == 1:
                return v
            if len(shape) == 2:
                return v.rearrange("p (a b) -> p a b", a=shape[0])
            if len(shape) == 3:
                return v.rearrange("p (a b c) -> p a b c", a=shape[0], b=shape[1])
            raise ValueError(shape)

    def rup(x):
        return ((x + 63) // 64) * 64
    CW5 = min(512, S)
    szA = max(2 * KC * S,
              4 * S + 4 * NJ * E + 2 * NJ * E + 64 + 4 * NJ * E,
              2 * (2 * GW + 64) + 4 * (2 * QCW * 2 + 64) + 4 * QCW + 2 * QCW + 256,
              4 * D * 4 + 2 * D * 2 + 256)
    szB = max(2 * (4 * D + 64) + 2 * (4 * D + 64) + 2 * (2 * D + 64)
              + 2 * (2 * KC * 128 + 64) + 2 * (4 * 512 + 64),
              2 * OC * S,
              2 * (2 * G5 * KC * 128 + 64) + 2 * (2 * G5 * D + 64))
    szC = max(4 * (4 * CW5 + 64) + 4 * CW5 + 4 * CW5 + 2 * CW5 + 4 * CW5 + 512,
              2 * OC * S,
              4 * (C // 128) * D,
              2 * OC * D)
    szE = max(4 * (2 * KC * 128 + 64) + 5 * (2 * S + 64),
              2 * KC * min(512, D) + 64,
              2 * OC * S)
    szF = max(2 * KC * C + 2 * (2 * D + 64) + 128,
              4 * max(NJ * E, NS, S) + 64 + 2 * 2048 * 2 + 4 * 4 * 128 + 512
              + 2 * (TALL // 16) + 2 * 2 * (NS + 16) + 128)
    regA = Region("regA", rup(szA))
    regB = Region("regB", rup(szB))
    regC = Region("regC", rup(szC))
    regE = Region("regE", rup(szE))
    regF = Region("regF", rup(szF))
    v_buf = sb("v_buf", (128, TC, H * 65), BF16)

    row8k = regF.f32(max(NJ * E, NS, S))[0:1, :]
    iop_sb = [regF.bf(2048)[0:16, :] for _ in range(2)]
    dDT_sb = regF.f32(128)[0:NJ, :]
    tokA_sb = regF.f32(128)[0:NSC, :]
    gridT_sb = regF.f32(128)
    extr_sb = regF.f32(128)[0:TC, :]
    didx_sb = regF.t[:, regF._take(2 * (TALL // 16)) // 2:][0:16, 0:TALL // 16].bitcast(I16)
    table_sb = regF.bf(2 * (NS + 16))[0:16, :]
    regF.reset()
    xn_fm = regA.bf(KC, S)
    x_sb = [regB.f32(D) for _ in range(2)]
    xnf = [regB.f32(D) for _ in range(2)]
    xn_tmb = [regB.bf(D) for _ in range(2)]
    xnlo = [regB.bf(KC, 128) for _ in range(2)]
    lgev_sb2 = [regB.f32(512)[0:16, :] for _ in range(2)]

    mu_sb = sb("mu_sb", (128, 1), F32)
    ssq_sb = sb("ssq_sb", (128, 1), F32)
    sqd_sb = sb("sqd_sb", (128, 1), F32)
    var_sb = sb("var_sb", (128, 1), F32)
    r_sb = sb("r_sb", (128, 1), F32)
    nmr_sb = sb("nmr_sb", (128, 1), F32)
    wgh_sb = sb("wgh_sb", (128, KC, E), BF16)
    wgl_sb = sb("wgl_sb", (128, KC, E), BF16)
    wgf_sb = sb("wgf_sb", (128, KC, E), F32)
    m_sb = sb("m_sb", (128, NJ), F32)
    s_sum = sb("s_sum", (128, NJ), F32)
    gsc_sb = sb("gsc_sb", (128, NJ), F32)
    tmpB = sb("tmpB", (128, NJ), F32)
    tmpC = sb("tmpC", (128, NJ), F32)
    idx_g = sb("idx_g", (128, NJ), F32)
    loc_g = sb("loc_g", (128, NJ), F32)
    val_g = sb("val_g", (128, NJ), F32)
    w_g = sb("w_g", (128, NJ), F32)
    dD_g = sb("dD_g", (128, NJ), F32)
    rR_g = sb("rR_g", (128, NJ), F32)
    tots_sb = sb("tots_sb", (NJ, E), F32)
    offs_sb = sb("offs_sb", (NJ, E), F32)
    dgi_sb = sb("dgi_sb", (128, NSC), I32)
    ridx_sb = sb("ridx_sb", (128, TC), I32)
    wown_sb = sb("wown_sb", (128, TC), F32)
    twoc_sb = sb("twoc_sb", (128, 1), F32)
    sown_sb = sb("sown_sb", (128, TC), F32)
    eyeF = sb("eyeF", (128, 128), F32)
    eyeB = sb("eyeB", (128, 128), BF16)
    trilI = sb("trilI", (128, 128), BF16)
    trilS = sb("trilS", (128, 128), F32)
    onesR = sb("onesR", (1, 128), F32)
    onesC = sb("onesC", (128, 1), BF16)
    rev_sb = sb("rev_sb", (128, E), F32)
    bp_sb = sb("bp_sb", (16, 1), F32)
    ba_sb = sb("ba_sb", (16, 1), F32)
    pa_sb = sb("pa_sb", (16, 4), F32)
    fr_sb = sb("fr_sb", (1, dh // 2), F32)

    def psum(name, shape, dt=F32):
        return es.enter_context(nc.psum_tensor(name, list(shape), dt))

    PBIG = psum("PBIG", (128, 2048))
    PA = psum("PA", (128, 512))
    PB = psum("PB", (128, 512))
    PC = psum("PC", (128, 512))
    PD = psum("PD", (128, 512))

    rec = Rec()

    def mksem(name):
        return Sem(nc, es, name)

    s_xl = mksem("s_xl"); s_cl = mksem("s_cl"); s_wl = mksem("s_wl")
    s_sc = mksem("s_sc"); s_vc = mksem("s_vc"); s_tc = mksem("s_tc")
    s_gp = mksem("s_gp"); s_xd = mksem("s_xd"); s_cc1 = mksem("s_cc1")
    s_cc2 = mksem("s_cc2"); s_cc3 = mksem("s_cc3"); s_gl = mksem("s_gl")
    s_io = mksem("s_io"); s_out = mksem("s_out")
    s_ms = mksem("s_ms"); s_td = mksem("s_td")
    s_iop = mksem("s_iop")
    s_wstr = [mksem(f"s_wstr{i}") for i in range(4)]
    s_w1p = [mksem(f"s_w1p{i}") for i in range(2)]
    s_y2p = [mksem(f"s_y2p{i}") for i in range(2)]
    s_igp = [mksem(f"s_igp{i}") for i in range(2)]
    s_iop2 = [mksem(f"s_iop2{i}") for i in range(2)]
    s_rdp = [mksem(f"s_rdp{i}") for i in range(4)]
    s_lgp = [mksem(f"s_lgp{i}") for i in range(2)]
    class RdPool:
        def __init__(self, sems):
            self.sems = sems
            self.i = 0
        def next(self):
            s = self.sems[self.i % len(self.sems)]
            self.i += 1
            return s
    rdpool = RdPool(s_rdp)
    s_outp = [mksem(f"s_outp{i}") for i in range(2)]
    s_tb = [mksem(f"s_tb{i}") for i in range(2)]
    s_xdp = [mksem(f"s_xdp{i}") for i in range(2)]

    def gp_barrier():
        sems = ([s_wl, s_iop, s_cc1, s_cc2, s_cc3, s_gp, s_ms]
                + s_wstr + s_w1p + s_igp + s_iop2 + s_rdp)
        return tuple((s, s.n) for s in sems if s.n > 0)

    E_ = rec.emit

    for dst, src in ((eyeF, eye_f32c), (eyeB, eye_bf16c), (trilI, tril_ic),
                     (trilS, tril_sc), (onesR, ones_rc), (onesC, ones_cc), (rev_sb, rev16c),
                     (twoc_sb, twoc_in), (sown_sb, sown_in), (wgf_sb, wg_in),
                     (bp_sb, bp_in), (ba_sb, ba_in), (fr_sb, fr_in)):
        E_("sync", lambda e, dst=dst, src=src: e.dma_start(
            out=dst[tuple([slice(None)] * len(dst.shape))], in_=src[tuple([slice(None)] * len(src.shape))]),
            inc=s_cl, inck=16)
    pos_sb = row8k[:, 0:S]
    E_("sync", lambda e: e.dma_start(out=pos_sb, in_=posc[:, :]), inc=s_cl, inck=16)
    CL = s_cl.n
    E_("gpsimd", lambda e: e.dma_start(out=wgh_sb[:, :, :], in_=wg_in[:, :, :]),
        inc=s_wl, inck=16)
    WL_G = s_wl.n
    E_("vector", lambda e: e.scalar_tensor_tensor(
        out=wgl_sb[:, :, :], in0=wgf_sb[:, :, :], scalar=1.0, in1=wgh_sb[:, :, :],
        op0=ALU.mult, op1=ALU.subtract),
        inc=s_vc, waits=((s_cl, CL), (s_wl, WL_G)))
    V_WGL = s_vc.n

    gate_started = set()
    ps_gate = PBIG[0:16, 0:S]
    xd_done = {}
    xn_done_v = []
    chunk_t = []
    chunk_s = []
    for tc in range(TC):
        bi = tc % 2
        wait_prev = ()
        if tc >= 2:
            wait_prev = ((s_sc.h_, chunk_s[tc - 2]) if False else (s_sc, chunk_s[tc - 2]),
                         (s_vc, xn_done_v[tc - 2]), (s_tc, chunk_t[tc - 2]))
        E_("sync", lambda e, tc=tc, bi=bi: e.dma_start(
            out=x_sb[bi][:, :], in_=x_in[tc * 128:(tc + 1) * 128, :]),
            inc=s_xl, inck=16, waits=wait_prev)
        XL = s_xl.n
        E_("vector", lambda e, bi=bi: e.tensor_reduce(
            out=mu_sb[:, :], in_=x_sb[bi][:, :], axis=AX.X, op=ALU.add),
            inc=s_vc, waits=((s_xl, XL), (s_sc, s_sc.n)))
        E_("scalar", lambda e, bi=bi: e.activation(
            out=sqd_sb[:, :], in_=x_sb[bi][:, 0:1], func=AF.Square,
            accum_out=ssq_sb[:, :]) if False else e.activation(
            out=xnf[bi][:, :], in_=x_sb[bi][:, :], func=AF.Square,
            accum_out=ssq_sb[:, :]),
            inc=s_sc, waits=((s_xl, XL), (s_vc, s_vc.n)))
        E_("vector", lambda e: e.tensor_scalar_mul(mu_sb[:, :], mu_sb[:, :], 1.0 / D),
            inc=s_vc, waits=(s_vc,))
        E_("vector", lambda e: e.tensor_tensor(
            out=var_sb[:, :], in0=mu_sb[:, :], in1=mu_sb[:, :], op=ALU.mult),
            inc=s_vc, waits=(s_vc,))
        E_("vector", lambda e: e.scalar_tensor_tensor(
            out=var_sb[:, :], in0=ssq_sb[:, :], scalar=1.0 / D, in1=var_sb[:, :],
            op0=ALU.mult, op1=ALU.subtract),
            inc=s_vc, waits=(s_vc, s_sc))
        E_("vector", lambda e: e.tensor_scalar_add(var_sb[:, :], var_sb[:, :], 1e-5),
            inc=s_vc, waits=(s_vc,))
        E_("scalar", lambda e: e.activation(out=var_sb[:, :], in_=var_sb[:, :],
                                            func=AF.Sqrt),
            inc=s_sc, waits=(s_vc,))
        E_("vector", lambda e: e.reciprocal(out=r_sb[:, :], in_=var_sb[:, :]),
            inc=s_vc, waits=(s_sc,))
        E_("vector", lambda e: e.tensor_tensor(
            out=nmr_sb[:, :], in0=mu_sb[:, :], in1=r_sb[:, :], op=ALU.mult),
            inc=s_vc, waits=(s_vc,))
        E_("vector", lambda e: e.tensor_scalar_mul(nmr_sb[:, :], nmr_sb[:, :], -1.0),
            inc=s_vc, waits=(s_vc,))
        V_STATS = s_vc.n
        E_("scalar", lambda e, bi=bi: e.activation(
            out=xnf[bi][:, :], in_=x_sb[bi][:, :], func=AF.Identity,
            bias=nmr_sb[:, :], scale=r_sb[:, :]),
            inc=s_sc, waits=((s_vc, V_STATS),))
        SC_XN = s_sc.n
        chunk_s.append(SC_XN)
        E_("vector", lambda e, bi=bi: e.tensor_copy(out=xn_tmb[bi][:, :], in_=xnf[bi][:, :]),
            inc=s_vc, waits=((s_sc, SC_XN), (s_xdp[bi], xd_done.get(tc - 2, 0)))),
        V_TMB = s_vc.n
        E_("sync", lambda e, tc=tc, bi=bi: e.dma_start(
            out=xn_dram.ap()[tc * 128:(tc + 1) * 128, :], in_=xn_tmb[bi][:, :]),
            inc=s_xdp[bi], inck=16, waits=((s_vc, V_TMB),))
        xd_done[tc] = s_xdp[bi].n
        for kc in range(KC):
            pt = [PA, PB][kc % 2]
            E_("tensor", lambda e, bi=bi, kc=kc, pt=pt: e.transpose(
                pt[:, 0:128], xnf[bi][:, kc * 128:(kc + 1) * 128], eyeF[:, :]),
                inc=s_tc, waits=((s_sc, SC_XN), (s_vc, s_vc.n)))
            T_NOW = s_tc.n
            E_("vector", lambda e, kc=kc, tc=tc, pt=pt: e.tensor_copy(
                out=xn_fm[:, kc, tc * 128:(tc + 1) * 128], in_=pt[:, 0:128]),
                inc=s_vc, waits=((s_tc, T_NOW),))
            E_("vector", lambda e, kc=kc, tc=tc, pt=pt, bi=bi: e.scalar_tensor_tensor(
                out=xnlo[bi][:, kc, :], in0=pt[:, 0:128], scalar=1.0,
                in1=xn_fm[:, kc, tc * 128:(tc + 1) * 128],
                op0=ALU.mult, op1=ALU.subtract),
                inc=s_vc, waits=(s_vc,))
        V_CHUNK = s_vc.n
        xn_done_v.append(V_CHUNK)
        for pas in range(4):
            for kc in range(KC):
                lhs = [wgh_sb, wgh_sb, wgl_sb, wgl_sb][pas]
                rhs_hi = pas in (0, 2)
                bank = (tc * 128 * 4) // 2048
                start = bank not in gate_started
                gate_started.add(bank)
                islast = ((tc % 4 == 3) or (tc == TC - 1)) and pas == 3 and kc == KC - 1
                E_("tensor", lambda e, tc=tc, kc=kc, lhs=lhs, rhs_hi=rhs_hi,
                   start=start, islast=islast, bi=bi: e.matmul(
                       ps_gate[:, tc * 128:(tc + 1) * 128],
                       lhs[:, kc, :],
                       xn_fm[:, kc, tc * 128:(tc + 1) * 128] if rhs_hi
                       else xnlo[bi][:, kc, :],
                       start=start, stop=islast),
                   inc=s_tc, waits=((s_vc, V_CHUNK), (s_vc, V_WGL)))
        chunk_t.append(s_tc.n)
    T_GATES = s_tc.n
    XD_ALL = tuple((s, s.n) for s in s_xdp)

    for c5 in range((S + 511) // 512):
        w = min(512, S - c5 * 512)
        slg = s_lgp[c5 % 2]
        E_("vector", lambda e, c5=c5, w=w: e.tensor_copy(
            out=lgev_sb2[c5 % 2][:, 0:w], in_=ps_gate[:, c5 * 512:c5 * 512 + w]),
            inc=s_vc, waits=((s_tc, T_GATES), (slg, slg.n)))
        V_LG = s_vc.n
        E_("sync", lambda e, c5=c5, w=w: e.dma_start(
            out=lg_dram.ap()[:, c5 * 512:c5 * 512 + w], in_=lgev_sb2[c5 % 2][:, 0:w]),
            inc=slg, inck=16, waits=((s_vc, V_LG),))
    E_("gpsimd", lambda e: e.collective_compute(
        "AllGather", ALU.bypass, replica_groups=[list(range(NC))],
        ins=[lg_dram.ap().opt()], outs=[ag_lg.ap().opt()]),
        inc=s_cc1, waits=tuple((s, s.n) for s in s_lgp))
    E_("gpsimd", lambda e: e.collective_compute(
        "AllGather", ALU.bypass, replica_groups=[list(range(NC))],
        ins=[xn_dram.ap().opt()], outs=[ag_xn.ap().opt()]),
        inc=s_cc2, waits=XD_ALL)

    regE.reset()
    wstr = [regE.bf(KC, 128) for _ in range(4)]
    qraw = regE.bf(S)
    crep = regE.bf(S)
    srep = regE.bf(S)
    ropeu = regE.bf(S)[0:32, :]
    ropev = regE.bf(S)[0:32, :]
    wv_half = regE.bf_at(0, KC, min(512, D))
    regC.reset()
    CWc = min(512, S)
    angc = regC.f32(CWc)[0:32, :]
    angr = regC.f32(CWc)[0:32, :]
    kint = regC.f32(CWc).bitcast(I32)[0:32, :]
    kf = regC.f32(CWc)[0:32, :]
    csb = regC.f32(CWc)[0:64, :]
    tlhf = regC.f32(CWc)[0:16, :]
    tlhb = regC.bf(CWc)[0:16, :]
    pos16 = regC.f32(CWc)[0:16, :]
    TWO_PI = float(2 * np.pi)
    E_("vector", lambda e: e.tensor_scalar_max(pa_sb[:, 0:1], ba_sb[:, :], 0.01),
        inc=s_vc, waits=((s_cl, CL),))
    E_("vector", lambda e: e.tensor_scalar(
        out=pa_sb[:, 1:2], in0=bp_sb[:, :], scalar1=0.01, scalar2=-1.0,
        op0=ALU.max, op1=ALU.mult), inc=s_vc, waits=(s_vc,))
    E_("vector", lambda e: e.tensor_scalar_mul(pa_sb[:, 2:3], pa_sb[:, 0:1], -1.0),
        inc=s_vc, waits=(s_vc,))
    E_("vector", lambda e: e.tensor_scalar(
        out=pa_sb[:, 3:4], in0=pa_sb[:, 0:1], scalar1=float(S - 1), scalar2=1.0,
        op0=ALU.mult, op1=ALU.add), inc=s_vc, waits=(s_vc,))
    V_PA = s_vc.n

    MAGIC = 8388608.0

    def range_reduce(dst0, shift):
        E_("vector", lambda e, shift=shift: e.tensor_scalar(
            out=angr, in0=angc, scalar1=1.0 / TWO_PI,
            scalar2=shift / TWO_PI, op0=ALU.mult, op1=ALU.add),
            inc=s_vc, waits=((s_vc, s_vc.n),))
        E_("vector", lambda e: e.tensor_scalar_add(angr, angr, MAGIC),
            inc=s_vc, waits=(s_vc,))
        E_("vector", lambda e: e.tensor_scalar_add(kf, angr, -MAGIC),
            inc=s_vc, waits=(s_vc,))
        E_("vector", lambda e, dst0=dst0: e.scalar_tensor_tensor(
            out=csb[dst0:dst0 + 32, :], in0=kf, scalar=-TWO_PI,
            in1=angc, op0=ALU.mult, op1=ALU.add),
            inc=s_vc, waits=(s_vc,))
        if shift:
            E_("vector", lambda e, dst0=dst0, shift=shift: e.tensor_scalar_add(
                csb[dst0:dst0 + 32, :], csb[dst0:dst0 + 32, :], shift),
                inc=s_vc, waits=(s_vc,))

    for cb in range(S // CWc):
        c0 = cb * CWc
        E_("sync", lambda e, c0=c0: e.dma_start(
            out=pos16, in_=AP(posc.tensor, c0, [[0, 16], [1, CWc]])),
            inc=s_cl, inck=16, waits=((s_sc, s_sc.n), (s_tc, T_GATES)))
        CL16 = s_cl.n
        E_("tensor", lambda e, c0=c0: e.matmul(
            PC[0:32, 0:CWc], fr_sb[:, :], pos_sb[:, c0:c0 + CWc],
            start=True, stop=True),
            inc=s_tc, waits=((s_cl, CL), (s_tc, T_GATES), (s_vc, s_vc.n)))
        T_ANG = s_tc.n
        E_("vector", lambda e: e.tensor_copy(out=angc, in_=PC[0:32, 0:CWc]),
            inc=s_vc, waits=((s_tc, T_ANG),))
        range_reduce(0, 0.0)
        range_reduce(32, float(np.pi / 2))
        V_CS = s_vc.n
        E_("scalar", lambda e: e.activation(out=csb, in_=csb, func=AF.Sin),
            inc=s_sc, waits=((s_vc, V_CS),))
        SC_CS = s_sc.n
        for q4 in range(4):
            E_("vector", lambda e, q4=q4, c0=c0: e.tensor_copy(
                out=crep[q4 * 32:(q4 + 1) * 32, c0:c0 + CWc], in_=csb[32:64, :]),
                inc=s_vc, waits=((s_sc, SC_CS),))
            E_("vector", lambda e, q4=q4, c0=c0: e.tensor_copy(
                out=srep[q4 * 32:(q4 + 1) * 32, c0:c0 + CWc], in_=csb[0:32, :]),
                inc=s_vc, waits=((s_sc, SC_CS),))
        E_("scalar", lambda e: e.activation(
            out=tlhf[0:H, :], in_=pos16[0:H, :], func=AF.Ln,
            bias=1.0, scale=pa_sb[0:H, 0:1]),
            inc=s_sc, waits=((s_vc, V_PA), (s_cl, CL16), (s_td, s_td.n)))
        E_("scalar", lambda e: e.mul(tlhf[0:H, :], tlhf[0:H, :], pa_sb[0:H, 1:2]),
            inc=s_sc, waits=(s_sc,))
        E_("scalar", lambda e: e.activation(out=tlhf[0:H, :], in_=tlhf[0:H, :],
                                            func=AF.Exp),
            inc=s_sc, waits=(s_sc,))
        E_("vector", lambda e: e.tensor_copy(out=tlhb[0:H, :], in_=tlhf[0:H, :]),
            inc=s_vc, waits=((s_sc, s_sc.n),))
        V_TB = s_vc.n
        E_("sync", lambda e, c0=c0: e.dma_start(
            out=tall_dram.ap()[:, S - 1 + c0:S - 1 + c0 + CWc], in_=tlhb[0:H, :]),
            inc=s_td, inck=16, waits=((s_vc, V_TB),))
        cwd = min(CWc, (S - 1) - c0)
        E_("scalar", lambda e: e.activation(
            out=tlhf[0:H, :], in_=pos16[0:H, :], func=AF.Ln,
            bias=pa_sb[0:H, 3:4], scale=pa_sb[0:H, 2:3]),
            inc=s_sc, waits=((s_vc, V_TB), (s_td, s_td.n)))
        E_("scalar", lambda e: e.mul(tlhf[0:H, :], tlhf[0:H, :], pa_sb[0:H, 1:2]),
            inc=s_sc, waits=(s_sc,))
        E_("scalar", lambda e: e.activation(out=tlhf[0:H, :], in_=tlhf[0:H, :],
                                            func=AF.Exp),
            inc=s_sc, waits=(s_sc,))
        E_("vector", lambda e: e.tensor_copy(out=tlhb[0:H, :], in_=tlhf[0:H, :]),
            inc=s_vc, waits=((s_sc, s_sc.n),))
        V_TB2 = s_vc.n
        if cwd > 0:
            E_("sync", lambda e, c0=c0, cwd=cwd: e.dma_start(
                out=tall_dram.ap()[:, c0:c0 + cwd], in_=tlhb[0:H, 0:cwd]),
                inc=s_td, inck=16, waits=((s_vc, V_TB2),))
    V_CSREP = s_vc.n
    TD_ALL = s_td.n

    regB.reset()
    q_fm = regB.bf(OC, S)
    regC.reset()
    k_fm = regC.bf(OC, S)

    wload_v = {}
    LG_ALL = tuple((s, s.n) for s in s_lgp)
    for wi, (w_in, dst_fm) in enumerate(((wq_in, q_fm), (wk_in, k_fm))):
        region_guard = LG_ALL if wi == 0 else ((s_td, TD_ALL),)
        for oc in range(OC):
            slot = (wi * OC + oc) % 4
            sbuf_w = wstr[slot]
            swl = s_wstr[slot]
            E_("gpsimd", lambda e, w_in=w_in, oc=oc, sbuf_w=sbuf_w: e.dma_start(
                out=sbuf_w[:, :, :], in_=w_in[:, oc, :, :]),
                inc=swl, inck=16,
                waits=((s_tc, wload_v.get((wi * OC + oc) - 4, 0)),))
            WL = (swl, swl.n)
            for t4 in range(NQC):
                pt = [PA, PB][t4 % 2]
                for kc in range(KC):
                    E_("tensor", lambda e, oc=oc, t4=t4, kc=kc, pt=pt, sbuf_w=sbuf_w: e.matmul(
                        pt[:, 0:QCW], sbuf_w[:, kc, :],
                        xn_fm[:, kc, t4 * QCW:(t4 + 1) * QCW],
                        start=(kc == 0), stop=(kc == KC - 1)),
                        inc=s_tc,
                        waits=((s_vc, xn_done_v[-1]), WL, (s_vc, s_vc.n)))
                T_MM = s_tc.n
                E_("vector", lambda e, t4=t4, pt=pt: e.tensor_copy(
                    out=qraw[:, t4 * QCW:(t4 + 1) * QCW], in_=pt[:, 0:QCW]),
                    inc=s_vc, waits=((s_tc, T_MM),))
            wload_v[wi * OC + oc] = s_tc.n
            V_QR = s_vc.n
            for q2 in range(2):
                b0 = q2 * 64
                t1s = slice(b0, b0 + 32)
                t2s = slice(b0 + 32, b0 + 64)
                E_("vector", lambda e, t1s=t1s: e.tensor_tensor(
                    out=ropeu, in0=qraw[t1s, :], in1=crep[t1s, :], op=ALU.mult),
                    inc=s_vc, waits=((s_vc, V_QR), (s_vc, V_CSREP), (s_vc, s_vc.n))
                    + region_guard)
                E_("vector", lambda e, t2s=t2s: e.tensor_tensor(
                    out=ropev, in0=qraw[t2s, :], in1=srep[t2s, :], op=ALU.mult),
                    inc=s_vc, waits=(s_vc,))
                E_("vector", lambda e, oc=oc, t1s=t1s, dst_fm=dst_fm: e.tensor_tensor(
                    out=dst_fm[t1s, oc, :], in0=ropeu, in1=ropev, op=ALU.subtract),
                    inc=s_vc, waits=(s_vc,))
                E_("vector", lambda e, t1s=t1s: e.tensor_tensor(
                    out=ropeu, in0=qraw[t1s, :], in1=srep[t1s, :], op=ALU.mult),
                    inc=s_vc, waits=(s_vc,))
                E_("vector", lambda e, t2s=t2s: e.tensor_tensor(
                    out=ropev, in0=qraw[t2s, :], in1=crep[t2s, :], op=ALU.mult),
                    inc=s_vc, waits=(s_vc,))
                E_("vector", lambda e, oc=oc, t2s=t2s, dst_fm=dst_fm: e.tensor_tensor(
                    out=dst_fm[t2s, oc, :], in0=ropeu, in1=ropev, op=ALU.add),
                    inc=s_vc, waits=(s_vc,))
    V_QK = s_vc.n
    E_("vector", lambda e: e.memset(
        v_buf[:, :, :].rearrange("p t (h u) -> p t h u", u=65)[:, :, :, 64:65], 1.0),
        inc=s_vc, waits=())
    V_VONES = s_vc.n
    VCW = min(512, D)
    for vc in range(D // VCW):
        E_("gpsimd", lambda e, vc=vc: e.dma_start(
            out=wv_half[:, :, :], in_=wv_in[:, :, vc * VCW:(vc + 1) * VCW]),
            inc=s_wl, inck=16, waits=((s_tc, s_tc.n),))
        WL_V = s_wl.n
        for tc in range(TC):
            pt = [PA, PB][tc % 2]
            for kc in range(KC):
                E_("tensor", lambda e, tc=tc, kc=kc, pt=pt: e.matmul(
                    pt[:, 0:VCW], xn_fm[:, kc, tc * 128:(tc + 1) * 128],
                    wv_half[:, kc, :],
                    start=(kc == 0), stop=(kc == KC - 1)),
                    inc=s_tc, waits=((s_wl, WL_V), (s_vc, s_vc.n)))
            T_MM = s_tc.n
            nh = VCW // 64
            E_("vector", lambda e, tc=tc, vc=vc, pt=pt, nh=nh: e.tensor_copy(
                out=v_buf[:, tc, :].rearrange("p (h u) -> p h u", u=65)
                [:, vc * nh:(vc + 1) * nh, 0:64],
                in_=pt[:, 0:VCW].rearrange("p (h u) -> p h u", u=64)),
                inc=s_vc, waits=((s_tc, T_MM), (s_vc, V_VONES)))
    V_V = s_vc.n
    T_P3 = s_tc.n

    regA.reset()
    L_sb = regA.f32(S)
    ltm = regA.f32(NJ * E)
    mask_sb = regA.bf(NJ * E)
    tmpA = regA.f32(NJ * E)
    rec_wait_cc1 = (s_cc1, 1)
    E_("sync", lambda e: e.dma_start(out=L_sb[0:NC * E, 0:S], in_=ag_lg.ap()[:, :]),
        inc=s_gl, inck=16, waits=(rec_wait_cc1, (s_vc, V_QK), (s_tc, T_P3)))
    GL = s_gl.n
    for m in range((NC * E) // 32):
        for cc in range(TC):
            pt = [PA, PB][(m * TC + cc) % 2]
            E_("tensor", lambda e, m=m, cc=cc, pt=pt: e.transpose(
                pt[:, 0:32], L_sb[m * 32:(m + 1) * 32, cc * 128:(cc + 1) * 128],
                eyeF[m * 32:(m + 1) * 32, m * 32:(m + 1) * 32],
                tile_position=(m * 32, 0)), inc=s_tc,
                waits=((s_gl, GL), (s_vc, s_vc.n)))
            T_NOW = s_tc.n
            for hh in range(2):
                j = (2 * m + hh) * TC + cc
                E_("vector", lambda e, pt=pt, hh=hh, j=j: e.tensor_copy(
                    out=ltm[:, j * E:(j + 1) * E], in_=pt[:, hh * 16:hh * 16 + E]),
                    inc=s_vc, waits=((s_tc, T_NOW),))
    V_LTM = s_vc.n
    ltm3 = ltm.rearrange("p (j e) -> p j e", e=E)
    tmpA3 = tmpA.rearrange("p (j e) -> p j e", e=E)
    mask3 = mask_sb.rearrange("p (j e) -> p j e", e=E)
    E_("vector", lambda e: e.tensor_reduce(out=m_sb[:, :], in_=ltm3, axis=AX.X,
                                           op=ALU.max),
        inc=s_vc, waits=((s_vc, V_LTM),))
    E_("vector", lambda e: e.tensor_tensor(
        out=tmpA3, in0=ltm3, in1=m_sb[:, :].unsqueeze(2).to_broadcast([128, NJ, E]),
        op=ALU.subtract), inc=s_vc, waits=(s_vc,))
    V_SUB = s_vc.n
    E_("scalar", lambda e: e.activation(out=tmpA, in_=tmpA, func=AF.Exp),
        inc=s_sc, waits=((s_vc, V_SUB),))
    SC_EXPG = s_sc.n
    E_("vector", lambda e: e.tensor_reduce(out=s_sum[:, :], in_=tmpA3, axis=AX.X,
                                           op=ALU.add),
        inc=s_vc, waits=((s_sc, SC_EXPG),))
    E_("vector", lambda e: e.reciprocal(out=gsc_sb[:, :], in_=s_sum[:, :]),
        inc=s_vc, waits=(s_vc,))
    V_GSC = s_vc.n
    E_("vector", lambda e: e.tensor_tensor(
        out=mask3, in0=ltm3, in1=m_sb[:, :].unsqueeze(2).to_broadcast([128, NJ, E]),
        op=ALU.is_ge), inc=s_vc, waits=(s_vc,))
    V_MASK = s_vc.n
    E_("vector", lambda e: e.tensor_tensor(
        out=tmpA3, in0=mask3, in1=rev_sb[:, :].unsqueeze(1).to_broadcast([128, NJ, E]),
        op=ALU.mult), inc=s_vc, waits=(s_vc,))
    E_("vector", lambda e: e.tensor_reduce(out=tmpB[:, :], in_=tmpA3, axis=AX.X,
                                           op=ALU.max),
        inc=s_vc, waits=(s_vc,))
    E_("vector", lambda e: e.tensor_scalar(
        out=idx_g[:, :], in0=tmpB[:, :], scalar1=-1.0, scalar2=float(E - 1),
        op0=ALU.mult, op1=ALU.add), inc=s_vc, waits=(s_vc,))
    V_IDX = s_vc.n
    ECW = min(512, NJ * E)
    cum_started = []
    for b in range((NJ * E) // ECW):
        E_("tensor", lambda e, b=b: e.matmul(
            PBIG[:, b * ECW:(b + 1) * ECW], trilI[:, :],
            mask_sb[:, b * ECW:(b + 1) * ECW], start=True, stop=False),
            inc=s_tc, waits=((s_vc, V_MASK), (s_cl, CL), (s_vc, s_vc.n)))
    T_CUM1 = s_tc.n
    for b in range((NJ * E + 511) // 512):
        w = min(512, NJ * E - b * 512)
        E_("tensor", lambda e, b=b, w=w: e.matmul(
            PA[0:1, 0:w], onesC[:, :], mask_sb[:, b * 512:b * 512 + w],
            start=True, stop=True),
            inc=s_tc, waits=((s_vc, V_MASK), (s_cl, CL), (s_vc, s_vc.n)))
        T_TOTB = s_tc.n
        E_("vector", lambda e, b=b, w=w: e.tensor_copy(
            out=row8k[:, b * 512:b * 512 + w], in_=PA[0:1, 0:w]),
            inc=s_vc, waits=((s_tc, T_TOTB),))
    V_TOT = s_vc.n
    rd1 = rdpool.next()
    E_("gpsimd", lambda e: e.dma_start(
        out=tot_dram.ap().unsqueeze(0),
        in_=row8k[:, 0:NJ * E].rearrange("q (j e) -> q j e", e=E)),
        inc=rd1, inck=16, waits=((s_vc, V_TOT),))
    rd2 = rdpool.next()
    E_("gpsimd", lambda e: e.dma_start(out=tots_sb[:, :], in_=tot_dram.ap()[:, :]),
        inc=rd2, inck=16, waits=(rd1,))
    RD_TOTS = (rd2, rd2.n)
    E_("tensor", lambda e: e.matmul(PA[0:NJ, 0:E], trilS[0:NJ, 0:NJ], tots_sb[0:NJ, :],
                                    start=True, stop=True),
        inc=s_tc, waits=(RD_TOTS, (s_vc, s_vc.n)))
    T_OFFS = s_tc.n
    E_("vector", lambda e: e.tensor_copy(out=offs_sb[:, :], in_=PA[0:NJ, 0:E]),
        inc=s_vc, waits=((s_tc, T_OFFS),))
    V_OFFS = s_vc.n
    rd3 = rdpool.next()
    E_("gpsimd", lambda e: e.dma_start(out=off_dram.ap()[:, :], in_=offs_sb[:, :]),
        inc=rd3, inck=16, waits=((s_vc, V_OFFS),))
    rd4 = rdpool.next()
    E_("gpsimd", lambda e: e.dma_start(
        out=row8k[:, 0:NJ * E].rearrange("q (j e) -> q j e", e=E),
        in_=off_dram.ap().unsqueeze(0)),
        inc=rd4, inck=16, waits=(rd3, (s_vc, s_vc.n)))
    RD_OFFR = (rd4, rd4.n)
    for b in range((NJ * E) // ECW):
        E_("tensor", lambda e, b=b: e.matmul(
            PBIG[:, b * ECW:(b + 1) * ECW], onesR[:, :],
            row8k[:, b * ECW:b * ECW + ECW], start=False, stop=True),
            inc=s_tc, waits=(RD_OFFR,))
    T_CUM2 = s_tc.n
    E_("vector", lambda e: e.tensor_tensor(
        out=tmpA, in0=PBIG[:, 0:NJ * E], in1=mask_sb[:, :], op=ALU.mult),
        inc=s_vc, waits=((s_tc, T_CUM2),))
    E_("vector", lambda e: e.tensor_reduce(out=loc_g[:, :], in_=tmpA3, axis=AX.X,
                                           op=ALU.add),
        inc=s_vc, waits=(s_vc,))
    E_("vector", lambda e: e.tensor_scalar_add(loc_g[:, :], loc_g[:, :], -1.0),
        inc=s_vc, waits=(s_vc,))
    V_LOC = s_vc.n
    E_("vector", lambda e: e.tensor_scalar(
        out=val_g[:, :], in0=loc_g[:, :], scalar1=float(C - 1), scalar2=None,
        op0=ALU.is_le), inc=s_vc, waits=(s_vc,))
    E_("vector", lambda e: e.tensor_tensor(out=w_g[:, :], in0=gsc_sb[:, :],
                                           in1=val_g[:, :], op=ALU.mult),
        inc=s_vc, waits=((s_vc, s_vc.n), (s_vc, V_GSC)))
    V_W = s_vc.n
    E_("vector", lambda e: e.tensor_scalar(
        out=tmpB[:, :], in0=idx_g[:, :], scalar1=twoc_sb[:, :], scalar2=None,
        op0=ALU.subtract), inc=s_vc, waits=((s_vc, V_IDX), (s_cl, CL)))
    E_("vector", lambda e: e.tensor_scalar(
        out=dD_g[:, :], in0=tmpB[:, :], scalar1=0.0, scalar2=None, op0=ALU.is_ge),
        inc=s_vc, waits=(s_vc,))
    E_("vector", lambda e: e.tensor_scalar(
        out=tmpC[:, :], in0=tmpB[:, :], scalar1=1.0, scalar2=None, op0=ALU.is_le),
        inc=s_vc, waits=(s_vc,))
    E_("vector", lambda e: e.tensor_tensor(out=dD_g[:, :], in0=dD_g[:, :],
                                           in1=tmpC[:, :], op=ALU.mult),
        inc=s_vc, waits=(s_vc,))
    E_("vector", lambda e: e.tensor_tensor(out=dD_g[:, :], in0=dD_g[:, :],
                                           in1=val_g[:, :], op=ALU.mult),
        inc=s_vc, waits=(s_vc,))
    E_("vector", lambda e: e.tensor_scalar(
        out=tmpB[:, :], in0=tmpB[:, :], scalar1=float(C), scalar2=None, op0=ALU.mult),
        inc=s_vc, waits=(s_vc,))
    E_("vector", lambda e: e.tensor_tensor(out=tmpB[:, :], in0=tmpB[:, :],
                                           in1=loc_g[:, :], op=ALU.add),
        inc=s_vc, waits=((s_vc, s_vc.n), (s_vc, V_LOC)))
    E_("vector", lambda e: e.tensor_scalar_add(tmpB[:, :], tmpB[:, :], -float(NS)),
        inc=s_vc, waits=(s_vc,))
    E_("vector", lambda e: e.tensor_tensor(out=tmpB[:, :], in0=tmpB[:, :],
                                           in1=dD_g[:, :], op=ALU.mult),
        inc=s_vc, waits=(s_vc,))
    E_("vector", lambda e: e.tensor_scalar_add(dD_g[:, :], tmpB[:, :], float(NS)),
        inc=s_vc, waits=(s_vc,))
    V_D = s_vc.n
    rd5 = rdpool.next()
    E_("gpsimd", lambda e: e.dma_start(out=dT_dram.ap()[:, :], in_=dD_g[:, :]),
        inc=rd5, inck=16, waits=((s_vc, V_D),))
    RD_DT = (rd5, rd5.n)
    rd5b = rdpool.next()
    E_("gpsimd", lambda e: e.dma_start(
        out=didx_sb[:, :].rearrange("q (t j) -> q t j", j=NJ),
        in_=dT_dram.ap().rearrange("(t q) j -> q t j", q=16)),
        inc=rd5b, inck=16, waits=(RD_DT,))
    RD_DIDX = (rd5b, rd5b.n)
    V_DIDX = s_vc.n
    E_("gpsimd", lambda e: e.memset(table_sb[:, :], 0.0), inc=s_ms,
       waits=gp_barrier())
    NCH = TALL // 1024
    for k in range(NCH):
        bi = k % 2
        E_("gpsimd", lambda e, k=k, bi=bi: e.dma_start(
            out=iop_sb[bi][:, :], in_=iopc[:, k * 2048:(k + 1) * 2048]),
            inc=s_iop, inck=16,
            waits=((s_gp, max(0, s_gp.n - 1)), (s_ms, s_ms.n)))
        IOP = s_iop.n
        E_("gpsimd", lambda e, k=k, bi=bi: e.scatter_add(
            table_sb[:, :].rearrange("p (n d) -> p n d", d=2),
            didx_sb[:, k * 64:(k + 1) * 64],
            iop_sb[bi][:, :].rearrange("p (n d) -> p n d", d=2),
            channels=16, num_elems=NS + 16, d=2, num_idxs=1024),
            inc=s_gp, waits=gp_barrier() + ((s_iop, IOP), (s_ms, s_ms.n),
                                            RD_DIDX, (s_cl, CL)))
    GP_SCAT = s_gp.n
    tab3 = table_sb[:, :].rearrange("p (n d) -> p n d", d=2)
    E_("vector", lambda e: e.scalar_tensor_tensor(
        out=row8k[:, 0:NS], in0=tab3[0:1, 0:NS, 0], scalar=128.0,
        in1=tab3[0:1, 0:NS, 1], op0=ALU.mult, op1=ALU.add),
        inc=s_vc, waits=((s_gp, GP_SCAT),))
    V_TOK = s_vc.n
    rd6 = rdpool.next()
    E_("gpsimd", lambda e: e.dma_start(
        out=tok_dram.ap().unsqueeze(0),
        in_=row8k[:, 0:NS].rearrange("q (c p) -> q c p", p=128)),
        inc=rd6, inck=16, waits=((s_vc, V_TOK),))
    rd7 = rdpool.next()
    E_("gpsimd", lambda e: e.dma_start(out=tokA_sb[:, :], in_=tok_dram.ap()[:, :]),
        inc=rd7, inck=16, waits=(rd6,))
    RD_TOK = (rd7, rd7.n)
    E_("tensor", lambda e: e.transpose(PA[:, 0:NSC],
                                       tokA_sb[0:NSC, :], eyeF[0:NSC, 0:NSC]),
        inc=s_tc, waits=(RD_TOK, (s_vc, s_vc.n)))
    T_TOK = s_tc.n
    E_("vector", lambda e: e.tensor_copy(out=dgi_sb[:, :], in_=PA[:, 0:NSC]),
        inc=s_vc, waits=((s_tc, T_TOK),))
    V_DGI = s_vc.n
    E_("vector", lambda e: e.tensor_scalar_min(tmpB[:, :], loc_g[:, :], float(C - 1)),
        inc=s_vc, waits=((s_vc, V_LOC), (s_vc, s_vc.n)))
    E_("vector", lambda e: e.scalar_tensor_tensor(
        out=rR_g[:, :], in0=idx_g[:, :], scalar=float(C), in1=tmpB[:, :],
        op0=ALU.mult, op1=ALU.add), inc=s_vc, waits=((s_vc, s_vc.n), (s_vc, V_IDX)))
    V_R = s_vc.n

    def extract_own(grid_sb, dst_sb, wv):
        E_("tensor", lambda e, grid_sb=grid_sb: e.transpose(
            PA[0:NJ, 0:128], grid_sb[:, 0:NJ], eyeF[:, :]),
            inc=s_tc, waits=((s_vc, wv), (s_vc, s_vc.n)))
        T1 = s_tc.n
        E_("vector", lambda e: e.tensor_copy(out=gridT_sb[0:NJ, :], in_=PA[0:NJ, 0:128]),
            inc=s_vc, waits=((s_tc, T1),))
        V1 = s_vc.n
        E_("tensor", lambda e: e.matmul(PB[0:TC, 0:128], sown_sb[0:NJ, :],
                                        gridT_sb[0:NJ, :], start=True, stop=True),
            inc=s_tc, waits=((s_vc, V1), (s_cl, CL)))
        T2 = s_tc.n
        E_("vector", lambda e: e.tensor_copy(out=extr_sb[:, :], in_=PB[0:TC, 0:128]),
            inc=s_vc, waits=((s_tc, T2),))
        V2 = s_vc.n
        E_("tensor", lambda e: e.transpose(PA[:, 0:TC], extr_sb[:, :],
                                           eyeF[0:TC, 0:TC]),
            inc=s_tc, waits=((s_vc, V2),))
        T3 = s_tc.n
        E_("vector", lambda e, dst_sb=dst_sb: e.tensor_copy(out=dst_sb[:, :],
                                                            in_=PA[:, 0:TC]),
            inc=s_vc, waits=((s_tc, T3),))
        return s_vc.n
    V_RIDX = extract_own(rR_g, ridx_sb, V_R)
    V_WOWN = extract_own(w_g, wown_sb, V_W)

    if "qk" in debug:
        d1 = dbg_out("qfm", (128, OC * S), BF16)
        d2 = dbg_out("kfm", (128, OC * S), BF16)
        E_("sync", lambda e, d1=d1: e.dma_start(
            out=d1[:, :], in_=q_fm[:, :, :].rearrange("p a b -> p (a b)")),
            inc=s_outp[0], inck=16, waits=((s_vc, V_QK),))
        E_("sync", lambda e, d2=d2: e.dma_start(
            out=d2[:, :], in_=k_fm[:, :, :].rearrange("p a b -> p (a b)")),
            inc=s_outp[0], inck=16, waits=((s_vc, V_QK),))
    if "xn" in debug:
        d = dbg_out("xn", (S, D), BF16)
        E_("sync", lambda e, d=d: e.dma_start(out=d[:, :], in_=xn_dram.ap()[:, :]),
            inc=s_outp[0], inck=16, waits=XD_ALL)
    if "lg" in debug:
        d = dbg_out("lg", (NC * E, S))
        E_("sync", lambda e, d=d: e.dma_start(out=d[:, :], in_=ag_lg.ap()[:, :]),
            inc=s_outp[0], inck=16, waits=((s_cc1, 1),))
    if "idx" in debug:
        d = dbg_out("idx", (128, NJ))
        E_("sync", lambda e, d=d: e.dma_start(out=d[:, :], in_=idx_g[:, :]),
            inc=s_outp[0], inck=16, waits=((s_vc, V_IDX),))
    if "loc" in debug:
        d = dbg_out("loc", (128, NJ))
        E_("sync", lambda e, d=d: e.dma_start(out=d[:, :], in_=loc_g[:, :]),
            inc=s_outp[0], inck=16, waits=((s_vc, V_LOC),))
    if "w" in debug:
        d = dbg_out("w", (128, NJ))
        E_("sync", lambda e, d=d: e.dma_start(out=d[:, :], in_=w_g[:, :]),
            inc=s_outp[0], inck=16, waits=((s_vc, V_W),))
    if "dgi" in debug:
        d = dbg_out("dgi", (128, NSC), I32)
        E_("sync", lambda e, d=d: e.dma_start(out=d[:, :], in_=dgi_sb[:, :]),
            inc=s_outp[0], inck=16, waits=((s_vc, V_DGI),))

    regA.reset()
    tbuf = [regA.bf(GW) for _ in range(2)]
    p0_sb = [regA.bf(2 * QCW) for _ in range(2)]
    pt_sb = [regA.bf(2, QCW) for _ in range(2)]
    linv_sb = regA.f32(QCW)[0:1, :]
    lb_sb = regA.bf(QCW)[0:64, :]
    regE.reset()
    attn_fm = regE.bf(OC, S)

    V_PRE4 = s_vc.n
    T_PRE4 = s_tc.n
    tb_v = {-2: V_PRE4, -1: V_PRE4}
    av_done = {}
    exp_s = {}
    mul_v = {}
    for h in range(H):
        oc = h // 2
        rb = (h % 2) * 64
        ti = h % 2
        E_("gpsimd", lambda e, h=h, ti=ti: e.dma_start(
            out=tbuf[ti],
            in_=AP(tall_dram.ap().tensor, h * TROW, [[1, 128], [1, GW]])),
            inc=s_tb[ti], inck=16, waits=((s_td, TD_ALL), (s_vc, tb_v.get(h - 2, 0)),
                                          (s_tc, T_PRE4), (s_gp, s_gp.n)))
        GP_TB = (s_tb[ti], s_tb[ti].n)
        for qc in range(NQC):
            q0 = qc * QCW
            for kcp in range(NKP):
                ri = kcp % 2
                scps = PBIG[:, ri * 1024:ri * 1024 + 2 * QCW]
                for half in range(2):
                    kc = kcp * 2 + half
                    E_("tensor", lambda e, oc=oc, rb=rb, kc=kc, half=half, scps=scps, q0=q0: e.matmul(
                        scps[:, half * QCW:(half + 1) * QCW],
                        k_fm[rb:rb + 64, oc, kc * 128:(kc + 1) * 128],
                        q_fm[rb:rb + 64, oc, q0:q0 + QCW],
                        start=True, stop=True),
                        inc=s_tc,
                        waits=((s_vc, V_QK),
                               (s_sc, exp_s.get((h, qc, kcp - 2), 0)),
                               (s_vc, max(mul_v.get((h, qc, kcp - 2), 0), V_PRE4))))
                T_SC = s_tc.n
                E_("scalar", lambda e, ri=ri, scps=scps: e.activation(
                    out=p0_sb[ri], in_=scps, func=AF.Exp, scale=SCALE),
                    inc=s_sc, waits=((s_tc, T_SC),))
                exp_s[(h, qc, kcp)] = s_sc.n
                for half in range(2):
                    kc = kcp * 2 + half
                    g0 = (S - 1) + kc * 128 - q0
                    FSA = regA.t.shape[1]
                    rev1 = AP(regA.t, tbuf[ti].offset + g0, [[FSA, 128], [-1, QCW]])
                    E_("vector", lambda e, ri=ri, half=half, rev1=rev1: e.tensor_tensor(
                        out=pt_sb[ri][:, half, :],
                        in0=p0_sb[ri][:, half * QCW:(half + 1) * QCW],
                        in1=rev1, op=ALU.mult),
                        inc=s_vc,
                        waits=((s_sc, exp_s[(h, qc, kcp)]), GP_TB,
                               (s_tc, av_done.get((h, qc, kcp - 2), 0))))
                mul_v[(h, qc, kcp)] = s_vc.n
                if "pt" in debug and h == 9 and qc == 0:
                    if "pt" not in dbg:
                        dbg_out("pt", (S, QCW), BF16)
                        dbg_out("p0", (S, QCW), BF16)
                    dpt = dbg["pt"]
                    dp0 = dbg["p0"]
                    E_("sync", lambda e, kcp=kcp, ri=ri, dpt=dpt: e.dma_start(
                        out=dpt[kcp * 256:(kcp + 1) * 256, :].rearrange(
                            "(a p) b -> p a b", a=2),
                        in_=pt_sb[ri][:, :, :]),
                        inc=s_outp[0], inck=16, waits=((s_vc, s_vc.n),))
                    E_("sync", lambda e, kcp=kcp, ri=ri, dp0=dp0: e.dma_start(
                        out=dp0[kcp * 256:(kcp + 1) * 256, :].rearrange(
                            "(a p) b -> p a b", a=2),
                        in_=p0_sb[ri][:, :].rearrange("p (a b) -> p a b", a=2)),
                        inc=s_outp[0], inck=16, waits=((s_vc, s_vc.n),))
                for half in range(2):
                    kc = kcp * 2 + half
                    E_("tensor", lambda e, h=h, kc=kc, ri=ri, half=half: e.matmul(
                        PA[0:65, 0:QCW], v_buf[:, kc, h * 65:(h + 1) * 65],
                        pt_sb[ri][:, half, :],
                        start=(kc == 0), stop=(kc == TC - 1)),
                        inc=s_tc,
                        waits=((s_vc, mul_v[(h, qc, kcp)]), (s_vc, V_V)))
                av_done[(h, qc, kcp)] = s_tc.n
            T_AV = s_tc.n
            E_("vector", lambda e: e.reciprocal(out=linv_sb[:, :], in_=PA[64:65, 0:QCW]),
                inc=s_vc, waits=((s_tc, T_AV),))
            V_LI = s_vc.n
            E_("tensor", lambda e: e.matmul(PB[0:64, 0:QCW], onesR[:, 0:64],
                                            linv_sb[:, :], start=True, stop=True),
                inc=s_tc, waits=((s_vc, V_LI),))
            T_LB = s_tc.n
            E_("vector", lambda e: e.tensor_copy(out=lb_sb[:, 0:QCW], in_=PB[0:64, 0:QCW]),
                inc=s_vc, waits=((s_tc, T_LB),))
            V_LBE = s_vc.n
            E_("vector", lambda e, rb=rb, oc=oc, q0=q0: e.tensor_tensor(
                out=attn_fm[rb:rb + 64, oc, q0:q0 + QCW], in0=PA[0:64, 0:QCW],
                in1=lb_sb[:, 0:QCW], op=ALU.mult),
                inc=s_vc, waits=((s_vc, V_LBE),))
            mul_v[(h, qc, NKP - 1)] = s_vc.n
        tb_v[h] = s_vc.n
    V_ATTN = s_vc.n
    if "attn" in debug:
        d = dbg_out("attn", (128, OC * S), BF16)
        E_("sync", lambda e, d=d: e.dma_start(
            out=d[:, :], in_=attn_fm.rearrange("p a b -> p (a b)")),
            inc=s_outp[0], inck=16, waits=((s_vc, V_ATTN),))

    G = G5
    NG = FC // G
    regB.reset()
    w1g = [regB.bf(G, KC, 128) for _ in range(2)]
    w2g = [regB.bf(G, D) for _ in range(2)]
    regC.reset()
    y2acc = regC.f32(C // 128, D)
    regF.reset()
    dispE = regF.bf(KC, C)
    xg_sb = [regF.bf(D) for _ in range(2)]
    h_sbs = [sb(f"h_sbs{i}", (128, min(256, C)), BF16) for i in range(2)]
    y2e_sb = [sb(f"y2e_sb{i}", (128, D), BF16) for i in range(2)]
    TGW = min(256, C)
    NTG = C // TGW
    NT2 = TGW // 128
    OW = min(512, D)

    w1v = {}
    grp_t = {}
    T_P4B = s_tc.n

    def emit_wload(gidx):
        e_i, g = gidx // NG, gidx % NG
        wb = gidx % 2
        sw = s_w1p[wb]
        E_("gpsimd", lambda e, e_i=e_i, g=g, wb=wb: e.dma_start(
            out=w1g[wb][:, :, :, :], in_=w1_in[e_i][:, g * G:(g + 1) * G, :, :]),
            inc=sw, inck=16, waits=((s_tc, max(grp_t.get(gidx - 2, 0), T_P4B)),))
        E_("gpsimd", lambda e, e_i=e_i, g=g, wb=wb: e.dma_start(
            out=w2g[wb][:, :, :], in_=w2_in[e_i][:, g * G:(g + 1) * G, :]),
            inc=sw, inck=16, waits=((s_tc, max(grp_t.get(gidx - 2, 0), T_P4B)),))
        w1v[gidx] = (sw, sw.n)

    emit_wload(0)
    emit_wload(1)
    ig_prev = {}
    for e_i in range(EPC):
        for c2 in range(C // 128):
            cglob = (e_i * C) // 128 + c2
            xgi = cglob % 2
            sig = s_igp[xgi]
            E_("gpsimd", lambda e, cglob=cglob, xgi=xgi: e.indirect_dma_start(
                out=xg_sb[xgi][:, :], out_offset=None,
                in_=ag_xn.ap()[:, :],
                in_offset=IndirectOffsetOnAxis(dgi_sb[:, cglob:cglob + 1], 0)),
                inc=sig, inck=16,
                waits=((s_cc2, 1), (s_vc, V_DGI),
                       (s_vc, ig_prev.get(cglob - 2, 0)), (s_gp, s_gp.n),
                       (s_tc, s_tc.n)))
            IG = (sig, sig.n)
            for kc in range(KC):
                pt = [PC, PD][kc % 2]
                E_("tensor", lambda e, xgi=xgi, kc=kc, pt=pt: e.transpose(
                    pt[:, 0:128].bitcast(BF16)[:, 0:128],
                    xg_sb[xgi][:, kc * 128:(kc + 1) * 128], eyeB[:, :]),
                    inc=s_tc, waits=(IG, (s_vc, s_vc.n), (s_cl, CL)))
                T_NOW = s_tc.n
                E_("vector", lambda e, kc=kc, c2=c2, pt=pt: e.tensor_copy(
                    out=dispE[:, kc, c2 * 128:(c2 + 1) * 128],
                    in_=pt[:, 0:128].bitcast(BF16)[:, 0:128]),
                    inc=s_vc, waits=((s_tc, T_NOW),))
            ig_prev[cglob] = s_vc.n
        V_DISP = s_vc.n
        for g in range(NG):
            gidx = e_i * NG + g
            wb = gidx % 2
            for tg in range(NTG):
                y2ps = [PBIG[:, 0:D], PBIG[:, 1024:1024 + D]]
                for fcl in range(G):
                    hb = fcl % 2
                    hps = [PC, PD][hb]
                    for kc in range(KC):
                        E_("tensor", lambda e, kc=kc, wb=wb, fcl=fcl, hps=hps,
                           tg=tg: e.matmul(
                               hps[:, 0:TGW], w1g[wb][:, fcl, kc, :],
                               dispE[:, kc, tg * TGW:(tg + 1) * TGW],
                               start=(kc == 0), stop=(kc == KC - 1)),
                            inc=s_tc,
                            waits=(w1v[gidx], (s_vc, V_DISP),
                                   (s_sc, s_sc.n)))
                    T_H = s_tc.n
                    E_("scalar", lambda e, hb=hb, hps=hps: e.activation(
                        out=h_sbs[hb][:, :], in_=hps[:, 0:TGW], func=AF.Relu),
                        inc=s_sc, waits=((s_tc, T_H),))
                    SC_H = s_sc.n
                    for tc2 in range(NT2):
                        for ocb in range(D // OW):
                            E_("tensor", lambda e, fcl=fcl, tc2=tc2, ocb=ocb,
                               wb=wb, hb=hb, y2ps=y2ps: e.matmul(
                                   y2ps[tc2][:, ocb * OW:(ocb + 1) * OW],
                                   h_sbs[hb][:, tc2 * 128:(tc2 + 1) * 128],
                                   w2g[wb][:, fcl, ocb * OW:(ocb + 1) * OW],
                                   start=(fcl == 0), stop=(fcl == G - 1)),
                                inc=s_tc,
                                waits=((s_sc, SC_H), (s_vc, s_vc.n) if fcl == 0
                                       else (s_sc, SC_H)))
                T_GRP = s_tc.n
                for tc2 in range(NT2):
                    row = tg * NT2 + tc2
                    if g == 0:
                        E_("vector", lambda e, tc2=tc2, row=row, y2ps=y2ps:
                           e.tensor_copy(out=y2acc[:, row, :], in_=y2ps[tc2][:, 0:D]),
                           inc=s_vc, waits=((s_tc, T_GRP),))
                    else:
                        E_("vector", lambda e, tc2=tc2, row=row, y2ps=y2ps:
                           e.tensor_tensor(out=y2acc[:, row, :], in0=y2ps[tc2][:, 0:D],
                                           in1=y2acc[:, row, :], op=ALU.add),
                           inc=s_vc, waits=((s_tc, T_GRP),))
            grp_t[gidx] = s_tc.n
            if gidx + 2 < EPC * NG:
                emit_wload(gidx + 2)
        V_Y2A = s_vc.n
        for row in range(C // 128):
            ybi = row % 2
            sy = s_y2p[ybi]
            E_("vector", lambda e, row=row, ybi=ybi: e.tensor_copy(
                out=y2e_sb[ybi][:, :], in_=y2acc[:, row, :]),
                inc=s_vc, waits=((s_vc, V_Y2A), (sy, sy.n)))
            V_Y2 = s_vc.n
            slot0 = e_i * C + row * 128
            E_("sync", lambda e, slot0=slot0, ybi=ybi: e.dma_start(
                out=y2_dram.ap()[slot0:slot0 + 128, :], in_=y2e_sb[ybi][:, :]),
                inc=sy, inck=16, waits=((s_vc, V_Y2),))
    Y2_ALL = [(s, s.n) for s in s_y2p]

    if "y2" in debug:
        d = dbg_out("y2", (NS, D), BF16)
        E_("sync", lambda e, d=d: e.dma_start(out=d[:, :], in_=y2_dram.ap()[:, :]),
            inc=s_outp[0], inck=16, waits=tuple(Y2_ALL))
    if "tall" in debug:
        d = dbg_out("tall", (H, TROW), BF16)
        E_("sync", lambda e, d=d: e.dma_start(out=d[:, :], in_=tall_dram.ap()[:, :]),
            inc=s_outp[0], inck=16, waits=((s_td, TD_ALL),))
    E_("gpsimd", lambda e: e.collective_compute(
        "AllGather", ALU.bypass, replica_groups=[list(range(NC))],
        ins=[y2_dram.ap().opt()], outs=[ag_y2.ap().opt()]),
        inc=s_cc3, waits=tuple(Y2_ALL))

    regC.reset()
    wo_sb = regC.bf(OC, D)
    E_("gpsimd", lambda e: e.dma_start(out=wo_sb[:, :, :], in_=wo_in[:, :, :]),
        inc=s_wl, inck=16, waits=((s_tc, s_tc.n), (s_vc, s_vc.n)))
    WL_O = (s_wl, s_wl.n)
    regA.reset()
    xr_sb = [regA.f32(D) for _ in range(2)]
    outs_sb = [regA.f32(D) for _ in range(2)]
    y2o_sb = [regA.bf(D) for _ in range(2)]
    out_done_v = {-2: s_vc.n, -1: s_vc.n}
    outst = {}
    T_PRE7 = s_tc.n
    for tc in range(TC):
        bi = tc % 2
        prjps = [PBIG[:, 0:D], PBIG[:, 1024:1024 + D]][bi]
        sio = s_iop2[bi]
        E_("gpsimd", lambda e, tc=tc, bi=bi: e.indirect_dma_start(
            out=y2o_sb[bi][:, :], out_offset=None,
            in_=ag_y2.ap()[:, :],
            in_offset=IndirectOffsetOnAxis(ridx_sb[:, tc:tc + 1], 0)),
            inc=sio, inck=16,
            waits=((s_cc3, 1), (s_vc, V_RIDX), (s_vc, out_done_v.get(tc - 2, 0)),
                   (s_tc, T_PRE7)))
        IO = (sio, sio.n)
        E_("sync", lambda e, tc=tc, bi=bi: e.dma_start(
            out=xr_sb[bi][:, :], in_=x_in[tc * 128:(tc + 1) * 128, :]),
            inc=s_xl, inck=16, waits=((s_vc, out_done_v.get(tc - 2, 0)),
                                      (s_tc, T_PRE7)))
        XR = s_xl.n
        OW = min(512, D)
        for oc in range(OC):
            for ocb in range(D // OW):
                E_("tensor", lambda e, oc=oc, ocb=ocb, tc=tc, OW=OW, prjps=prjps: e.matmul(
                    prjps[:, ocb * OW:(ocb + 1) * OW],
                    attn_fm[:, oc, tc * 128:(tc + 1) * 128],
                    wo_sb[:, oc, ocb * OW:(ocb + 1) * OW],
                    start=(oc == 0), stop=(oc == OC - 1)),
                    inc=s_tc,
                    waits=((s_vc, V_ATTN), WL_O,
                           (s_vc, out_done_v.get(tc - 2, 0))))
        T_PRJ = s_tc.n
        E_("vector", lambda e, bi=bi, prjps=prjps: e.tensor_tensor(
            out=outs_sb[bi][:, :], in0=prjps[:, 0:D], in1=xr_sb[bi][:, :], op=ALU.add),
            inc=s_vc, waits=((s_tc, T_PRJ), (s_xl, XR),
                             (s_outp[bi], outst.get(tc - 2, 0))))
        E_("vector", lambda e, bi=bi, tc=tc: e.scalar_tensor_tensor(
            out=outs_sb[bi][:, :], in0=y2o_sb[bi][:, :], scalar=wown_sb[:, tc:tc + 1],
            in1=outs_sb[bi][:, :], op0=ALU.mult, op1=ALU.add),
            inc=s_vc, waits=((s_vc, s_vc.n), IO, (s_vc, V_WOWN)))
        V_OUT = s_vc.n
        out_done_v[tc] = V_OUT
        so = s_outp[bi]
        E_("sync", lambda e, tc=tc, bi=bi: e.dma_start(
            out=out_t[tc * 128:(tc + 1) * 128, :], in_=outs_sb[bi][:, :]),
            inc=so, inck=16, waits=((s_vc, V_OUT),))
        outst[tc] = so.n
    def _final(e, tot=[(s.h, s.n) for s in s_outp]):
        for h, v in tot:
            e.wait_ge(h, v)
    rec.ops["sync"].append(_final)

    with nc.Block() as block:
        rec.replay(block)
    es.close()
    nc.finalize()
    return nc, dbg


def host_inputs(cfg: Cfg, inputs):
    S, D, H, dh, F_, NC, E = (cfg.S, cfg.D, cfg.H, cfg.dh, cfg.F, cfg.NC, cfg.E)
    KC, FC, OC, TC, EPC = cfg.KC, cfg.FC, cfg.OC, cfg.TC, cfg.EPC
    x = np.ascontiguousarray(inputs["x"], dtype=np.float32)
    qkv_w = np.asarray(inputs["qkv_w"], dtype=np.float32)
    out_w = np.asarray(inputs["out_w"], dtype=np.float32)
    bias_p = np.asarray(inputs["bias_p"], dtype=np.float32).reshape(H, 1)
    bias_a = np.asarray(inputs["bias_a"], dtype=np.float32).reshape(H, 1)
    freqs = np.asarray(inputs["rope_freqs"], dtype=np.float32).reshape(1, dh // 2)
    Wg = np.asarray(inputs["Wg"], dtype=np.float32)
    w1 = np.asarray(inputs["w1"], dtype=np.float32)
    w2 = np.asarray(inputs["w2"], dtype=np.float32)
    HD = H * dh

    def tile_lhsT(wT):
        t = wT.reshape(KC, 128, OC, 128)
        return np.ascontiguousarray(t.transpose(1, 2, 0, 3))

    wq_t = tile_lhsT(qkv_w[0:HD].T)
    wk_t = tile_lhsT(qkv_w[HD:2 * HD].T)
    wv_t = np.ascontiguousarray(qkv_w[2 * HD:3 * HD].T.reshape(KC, 128, HD)
                                .transpose(1, 0, 2))
    wo_t = np.ascontiguousarray(out_w.T.reshape(OC, 128, D).transpose(1, 0, 2))
    wg_t = np.ascontiguousarray(Wg.reshape(KC, 128, E).transpose(1, 0, 2))
    bp16 = np.zeros((16, 1), np.float32); bp16[:H] = bias_p
    ba16 = np.zeros((16, 1), np.float32); ba16[:H] = bias_a
    in_maps = []
    for c in range(NC):
        m = {
            "x": x[c], "wq": wq_t, "wk": wk_t, "wv": wv_t, "wo": wo_t,
            "wg": wg_t, "bp": bp16, "ba": ba16, "fr": freqs,
            "twoc": np.full((128, 1), 2.0 * c, np.float32),
        }
        so = np.zeros((128, TC), np.float32)
        for jj in range(TC):
            so[c * TC + jj, jj] = 1.0
        m["sown"] = so
        for e_i in range(EPC):
            ww1 = w1[c * EPC + e_i].reshape(KC, 128, FC, 128)
            m[f"w1_{e_i}"] = np.ascontiguousarray(ww1.transpose(1, 2, 0, 3))
            ww2 = w2[c * EPC + e_i].reshape(FC, 128, D)
            m[f"w2_{e_i}"] = np.ascontiguousarray(ww2.transpose(1, 0, 2))
        in_maps.append(m)
    return in_maps



def kernel(L0_attn_x, L0_attn_qkv_w, L0_attn_out_w, L0_attn_bias_p,
           L0_attn_bias_a, L0_attn_rope_freqs, L0_ffn_Wg, L0_ffn_w1, L0_ffn_w2):
    from concourse.bass_utils import run_bass_kernel_spmd
    cfg = FULL
    inputs = {
        "x": np.asarray(L0_attn_x), "qkv_w": np.asarray(L0_attn_qkv_w),
        "out_w": np.asarray(L0_attn_out_w), "bias_p": np.asarray(L0_attn_bias_p),
        "bias_a": np.asarray(L0_attn_bias_a),
        "rope_freqs": np.asarray(L0_attn_rope_freqs),
        "Wg": np.asarray(L0_ffn_Wg), "w1": np.asarray(L0_ffn_w1),
        "w2": np.asarray(L0_ffn_w2),
    }
    in_maps = host_inputs(cfg, inputs)
    nc, _ = build(cfg)
    res = run_bass_kernel_spmd(nc, in_maps, core_ids=list(range(cfg.NC)))
    return np.stack([r["out"] for r in res.results]).astype(np.float32)


# version 03 (final)
# speedup vs baseline: 1.1875x; 1.1875x over previous
import sys
sys.path.insert(0, '/opt/trn_rl_repo')
from contextlib import ExitStack
from dataclasses import dataclass

import numpy as np
import ml_dtypes

import concourse.mybir as mybir
from concourse.ap import AP
from concourse.bacc import Bacc
from concourse.bass_core_rust import IndirectOffsetOnAxis

F32 = mybir.dt.float32
BF16 = mybir.dt.bfloat16
I32 = mybir.dt.int32
I16 = mybir.dt.int16
AF = mybir.ActivationFunctionType
ALU = mybir.AluOpType
AX = mybir.AxisListType
BF = ml_dtypes.bfloat16


@dataclass(frozen=True)
class Cfg:
    S: int = 2048
    D: int = 1024
    H: int = 16
    dh: int = 64
    F: int = 4096
    NC: int = 8
    E: int = 16

    @property
    def KC(self): return self.D // 128
    @property
    def TC(self): return self.S // 128
    @property
    def EPC(self): return self.E // self.NC
    @property
    def C(self): return self.NC * self.S // self.E
    @property
    def NS(self): return self.EPC * self.C
    @property
    def NJ(self): return self.NC * self.TC
    @property
    def QCW(self): return min(512, self.S)
    @property
    def NQC(self): return self.S // self.QCW
    @property
    def FC(self): return self.F // 128
    @property
    def TALL(self): return self.NC * self.S
    @property
    def OC(self): return (self.H * self.dh) // 128
    @property
    def NSC(self): return self.NS // 128
    @property
    def GW(self): return 2 * self.S - 128


FULL = Cfg()
MINI = Cfg(S=256, D=256, H=4, F=512)
MIDI = Cfg(S=1024, D=256, H=4, F=512)
MIDI2 = Cfg(S=1024, D=1024, H=16, F=512)
MIDI3 = Cfg(S=2048, D=256, H=4, F=512)


class Sem:
    def __init__(self, nc, es, name):
        self.h = es.enter_context(nc.semaphore(name))
        self.n = 0


class Rec:
    ENGINES = ("sync", "gpsimd", "scalar", "vector", "tensor")

    def __init__(self):
        self.ops = {e: [] for e in self.ENGINES}

    def emit(self, eng, make, inc=None, inck=1, waits=()):
        ws = []
        for w in waits:
            if isinstance(w, Sem):
                ws.append((w.h, w.n))
            else:
                s, v = w
                ws.append((s.h, v))
        tgt = None
        if inc is not None:
            inc.n += inck
            tgt = inc.h

        def fn(e, ws=ws, make=make, tgt=tgt, k=inck):
            for h, v in ws:
                if v > 0:
                    e.wait_ge(h, v)
            inst = make(e)
            if tgt is not None and inst is not None:
                inst.then_inc(tgt, k)
        self.ops[eng].append(fn)

    def replay(self, block):
        for name in self.ENGINES:
            fns = self.ops[name]

            def run(eng, fns=fns):
                for f in fns:
                    f(eng)
            getattr(block, name)(run)


def build(cfg: Cfg, debug=()):
    S, D, H, dh, F_, NC, E = cfg.S, cfg.D, cfg.H, cfg.dh, cfg.F, cfg.NC, cfg.E
    KC, TC, EPC, C, NS, NJ, QCW, NQC, FC, TALL, OC, NSC, GW = (
        cfg.KC, cfg.TC, cfg.EPC, cfg.C, cfg.NS, cfg.NJ, cfg.QCW, cfg.NQC,
        cfg.FC, cfg.TALL, cfg.OC, cfg.NSC, cfg.GW)
    SCALE = 1.0 / float(np.sqrt(dh))
    TROW = 2 * S - 1
    G5 = 4
    NW = NJ // 16
    NKP = TC // 2
    TGW = min(256, C)
    NTG = C // TGW
    NT2 = TGW // 128

    nc = Bacc("TRN2", target_bir_lowering=False)

    def inp(name, shape, dt=F32):
        return nc.dram_tensor(name, list(shape), dt, kind="ExternalInput").ap()

    x_in = inp("x", (S, D))
    wq_in = inp("wq", (128, OC, KC, 128))
    wk_in = inp("wk", (128, OC, KC, 128))
    wv_in = inp("wv", (128, KC, D))
    wo_in = inp("wo", (128, OC, D))
    wg_in = inp("wg", (128, KC, E))
    w1_in = [inp(f"w1_{e}", (128, FC, KC, 128)) for e in range(EPC)]
    w2_in = [inp(f"w2_{e}", (128, FC, D)) for e in range(EPC)]
    bp_in = inp("bp", (16, 1))
    ba_in = inp("ba", (16, 1))
    fr_in = inp("fr", (1, dh // 2))
    twoc_in = inp("twoc", (128, 1))
    sown_in = inp("sown", (128, TC))

    out_t = nc.dram_tensor("out", [S, D], F32, kind="ExternalOutput").ap()
    dbg = {}

    def dbg_out(name, shape, dt=F32):
        d = nc.dram_tensor("dbg_" + name, list(shape), dt, kind="ExternalOutput")
        dbg[name] = d.ap()
        return dbg[name]

    def const(name, arr):
        return nc.inline_tensor(np.ascontiguousarray(arr), name).ap()

    eye_f32c = const("eye_f32c", np.eye(128, dtype=np.float32))
    eye_bf16c = const("eye_bf16c", np.eye(128, dtype=np.float32).astype(BF))
    tril_ic = const("tril_ic", np.triu(np.ones((128, 128), np.float32)).astype(BF))
    tril_sc = const("tril_sc", np.triu(np.ones((128, 128), np.float32), 1))
    ones_rc = const("ones_rc", np.ones((1, 128), np.float32))
    ones_cc = const("ones_cc", np.ones((128, 1), np.float32).astype(BF))
    rev16c = const("rev16c", np.broadcast_to(
        (E - 1.0) - np.arange(E, dtype=np.float32), (128, E)).copy())
    posc = const("posc", np.arange(S, dtype=np.float32).reshape(1, S))
    _i = np.arange(TALL)
    _scol = _i // 16
    _qe = _i % 16
    _j = _scol % NJ
    _t = _scol // NJ
    _gv = 128 * _j + 16 * _t + _qe
    iopc = const("iopc", np.broadcast_to(
        np.stack([_gv // 128, _gv % 128], 1).reshape(1, -1).astype(np.float32),
        (16, 2 * TALL)).astype(BF).copy())

    def idram(name, shape, dt):
        return nc.dram_tensor(name, list(shape), dt, kind="Internal")

    xn_dram = idram("xn_dram", (S, D), BF16)
    ag_xn = idram("ag_xn", (TALL, D), BF16)
    lg_dram = idram("lg_dram", (E, S), F32)
    ag_lg = idram("ag_lg", (NC * E, S), F32)
    dT_dram = idram("dT_dram", (128, NJ), F32)
    tot_dram = idram("tot_dram", (NJ, E), F32)
    off_dram = idram("off_dram", (NJ, E), F32)
    tok_dram = idram("tok_dram", (NSC, 128), F32)
    tall_dram = idram("tall_dram", (H, TROW), BF16)
    y2_dram = idram("y2_dram", (NS, D), BF16)
    ag_y2 = idram("ag_y2", (NC * NS, D), BF16)

    es = ExitStack()

    def sb(name, shape, dt):
        return es.enter_context(nc.sbuf_tensor(name, list(shape), dt))

    class Region:
        def __init__(self, name, nbytes):
            self.t = sb(name, (128, nbytes // 2), BF16)
            self.off = 0

        def reset(self):
            self.off = 0

        def _take(self, nbytes):
            o = self.off
            self.off += ((nbytes + 31) // 32) * 32
            assert self.off <= self.t.shape[1] * 2, \
                f"{self.t.name} overflow {self.off}"
            return o

        def bf_at(self, off, *shape):
            saved = self.off
            self.off = off
            v = self.bf(*shape)
            self.off = max(saved, self.off)
            return v

        def bf(self, *shape):
            n = int(np.prod(shape))
            o = self._take(2 * n)
            v = self.t[:, o // 2:o // 2 + n]
            return self._shape(v, shape)

        def f32(self, *shape):
            n = int(np.prod(shape))
            o = self._take(4 * n)
            v = self.t[:, o // 2:o // 2 + 2 * n].bitcast(F32)
            return self._shape(v, shape)

        @staticmethod
        def _shape(v, shape):
            if len(shape) == 1:
                return v
            if len(shape) == 2:
                return v.rearrange("p (a b) -> p a b", a=shape[0])
            if len(shape) == 3:
                return v.rearrange("p (a b c) -> p a b c", a=shape[0], b=shape[1])
            raise ValueError(shape)

    def rup(x):
        return ((x + 63) // 64) * 64
    CW5 = min(512, S)
    szA = max(2 * KC * S,
              4 * S + 4 * NJ * E + 2 * NJ * E + 64 + 4 * NJ * E,
              2 * (2 * GW + 64) + 4 * (2 * QCW * 2 + 64) + 4 * QCW + 2 * QCW + 256,
              4 * D * 4 + 2 * D * 2 + 256)
    szB = max(2 * (4 * D + 64) + 2 * (4 * D + 64) + 2 * (2 * D + 64)
              + 2 * (2 * KC * 128 + 64) + 2 * (4 * 512 + 64),
              2 * OC * S,
              2 * (2 * G5 * KC * 128 + 64) + 2 * (2 * G5 * D + 64))
    szC = max(4 * (4 * CW5 + 64) + 4 * CW5 + 4 * CW5 + 2 * CW5 + 4 * CW5 + 512,
              2 * OC * S,
              4 * (C // 128) * D,
              2 * OC * D)
    szE = max(4 * (2 * KC * 128 + 64) + 5 * (2 * S + 64),
              2 * KC * min(512, D) + 64,
              2 * OC * S)
    szF = max(2 * KC * C + 2 * (2 * D + 64) + 128,
              4 * max(NJ * E, NS, S) + 64 + 2 * 2048 * 2 + 4 * 4 * 128 + 512
              + 2 * (TALL // 16) + 2 * 2 * (NS + 16) + 128)
    regA = Region("regA", rup(szA))
    regB = Region("regB", rup(szB))
    regC = Region("regC", rup(szC))
    regE = Region("regE", rup(szE))
    regF = Region("regF", rup(szF))
    v_buf = sb("v_buf", (128, TC, H * 65), BF16)

    row8k = regF.f32(max(NJ * E, NS, S))[0:1, :]
    iop_sb = [regF.bf(2048)[0:16, :] for _ in range(2)]
    dDT_sb = regF.f32(128)[0:NJ, :]
    tokA_sb = regF.f32(128)[0:NSC, :]
    gridT_sb = regF.f32(128)
    extr_sb = regF.f32(128)[0:TC, :]
    didx_sb = regF.t[:, regF._take(2 * (TALL // 16)) // 2:][0:16, 0:TALL // 16].bitcast(I16)
    table_sb = regF.bf(2 * (NS + 16))[0:16, :]
    regF.reset()
    xn_fm = regA.bf(KC, S)
    x_sb = [regB.f32(D) for _ in range(2)]
    xnf = [regB.f32(D) for _ in range(2)]
    xn_tmb = [regB.bf(D) for _ in range(2)]
    xnlo = [regB.bf(KC, 128) for _ in range(2)]
    lgev_sb2 = [regB.f32(512)[0:16, :] for _ in range(2)]

    mu_sb = sb("mu_sb", (128, 1), F32)
    ssq_sb = sb("ssq_sb", (128, 1), F32)
    sqd_sb = sb("sqd_sb", (128, 1), F32)
    var_sb = sb("var_sb", (128, 1), F32)
    r_sb = sb("r_sb", (128, 1), F32)
    nmr_sb = sb("nmr_sb", (128, 1), F32)
    wgh_sb = sb("wgh_sb", (128, KC, E), BF16)
    wgl_sb = sb("wgl_sb", (128, KC, E), BF16)
    wgf_sb = sb("wgf_sb", (128, KC, E), F32)
    m_sb = sb("m_sb", (128, NJ), F32)
    s_sum = sb("s_sum", (128, NJ), F32)
    gsc_sb = sb("gsc_sb", (128, NJ), F32)
    tmpB = sb("tmpB", (128, NJ), F32)
    tmpC = sb("tmpC", (128, NJ), F32)
    idx_g = sb("idx_g", (128, NJ), F32)
    loc_g = sb("loc_g", (128, NJ), F32)
    val_g = sb("val_g", (128, NJ), F32)
    w_g = sb("w_g", (128, NJ), F32)
    dD_g = sb("dD_g", (128, NJ), F32)
    rR_g = sb("rR_g", (128, NJ), F32)
    tots_sb = sb("tots_sb", (NJ, E), F32)
    offs_sb = sb("offs_sb", (NJ, E), F32)
    dgi_sb = sb("dgi_sb", (128, NSC), I32)
    ridx_sb = sb("ridx_sb", (128, TC), I32)
    wown_sb = sb("wown_sb", (128, TC), F32)
    twoc_sb = sb("twoc_sb", (128, 1), F32)
    sown_sb = sb("sown_sb", (128, TC), F32)
    eyeF = sb("eyeF", (128, 128), F32)
    eyeB = sb("eyeB", (128, 128), BF16)
    trilI = sb("trilI", (128, 128), BF16)
    trilS = sb("trilS", (128, 128), F32)
    onesR = sb("onesR", (1, 128), F32)
    onesC = sb("onesC", (128, 1), BF16)
    rev_sb = sb("rev_sb", (128, E), F32)
    bp_sb = sb("bp_sb", (16, 1), F32)
    ba_sb = sb("ba_sb", (16, 1), F32)
    pa_sb = sb("pa_sb", (16, 4), F32)
    fr_sb = sb("fr_sb", (1, dh // 2), F32)

    def psum(name, shape, dt=F32):
        return es.enter_context(nc.psum_tensor(name, list(shape), dt))

    PBIG = psum("PBIG", (128, 2048))
    PA = psum("PA", (128, 512))
    PB = psum("PB", (128, 512))
    PC = psum("PC", (128, 512))
    PD = psum("PD", (128, 512))

    rec = Rec()

    def mksem(name):
        return Sem(nc, es, name)

    s_xl = mksem("s_xl"); s_cl = mksem("s_cl"); s_wl = mksem("s_wl")
    s_sc = mksem("s_sc"); s_vc = mksem("s_vc"); s_tc = mksem("s_tc")
    s_gp = mksem("s_gp"); s_xd = mksem("s_xd"); s_cc1 = mksem("s_cc1")
    s_cc2 = mksem("s_cc2"); s_cc3 = mksem("s_cc3"); s_gl = mksem("s_gl")
    s_io = mksem("s_io"); s_out = mksem("s_out")
    s_ms = mksem("s_ms"); s_td = mksem("s_td")
    s_iop = mksem("s_iop")
    s_wstr = [mksem(f"s_wstr{i}") for i in range(4)]
    s_w1p = [mksem(f"s_w1p{i}") for i in range(2)]
    s_y2p = [mksem(f"s_y2p{i}") for i in range(2)]
    s_igp = [mksem(f"s_igp{i}") for i in range(2)]
    s_iop2 = [mksem(f"s_iop2{i}") for i in range(2)]
    s_rdp = [mksem(f"s_rdp{i}") for i in range(4)]
    s_lgp = [mksem(f"s_lgp{i}") for i in range(2)]
    class RdPool:
        def __init__(self, sems):
            self.sems = sems
            self.i = 0
        def next(self):
            s = self.sems[self.i % len(self.sems)]
            self.i += 1
            return s
    rdpool = RdPool(s_rdp)
    s_outp = [mksem(f"s_outp{i}") for i in range(2)]
    s_tb = [mksem(f"s_tb{i}") for i in range(2)]
    s_xdp = [mksem(f"s_xdp{i}") for i in range(2)]

    def gp_barrier():
        sems = ([s_wl, s_iop, s_cc1, s_cc2, s_cc3, s_gp, s_ms]
                + s_wstr + s_w1p + s_igp + s_iop2 + s_rdp)
        return tuple((s, s.n) for s in sems if s.n > 0)

    E_ = rec.emit

    for dst, src in ((eyeF, eye_f32c), (eyeB, eye_bf16c), (trilI, tril_ic),
                     (trilS, tril_sc), (onesR, ones_rc), (onesC, ones_cc), (rev_sb, rev16c),
                     (twoc_sb, twoc_in), (sown_sb, sown_in), (wgf_sb, wg_in),
                     (bp_sb, bp_in), (ba_sb, ba_in), (fr_sb, fr_in)):
        E_("sync", lambda e, dst=dst, src=src: e.dma_start(
            out=dst[tuple([slice(None)] * len(dst.shape))], in_=src[tuple([slice(None)] * len(src.shape))]),
            inc=s_cl, inck=16)
    pos_sb = row8k[:, 0:S]
    E_("sync", lambda e: e.dma_start(out=pos_sb, in_=posc[:, :]), inc=s_cl, inck=16)
    CL = s_cl.n
    E_("gpsimd", lambda e: e.dma_start(out=wgh_sb[:, :, :], in_=wg_in[:, :, :]),
        inc=s_wl, inck=16)
    WL_G = s_wl.n
    E_("vector", lambda e: e.scalar_tensor_tensor(
        out=wgl_sb[:, :, :], in0=wgf_sb[:, :, :], scalar=1.0, in1=wgh_sb[:, :, :],
        op0=ALU.mult, op1=ALU.subtract),
        inc=s_vc, waits=((s_cl, CL), (s_wl, WL_G)))
    V_WGL = s_vc.n

    gate_started = set()
    ps_gate = PBIG[0:16, 0:S]
    xd_done = {}
    xn_done_v = []
    chunk_t = []
    chunk_s = []
    for tc in range(TC):
        bi = tc % 2
        wait_prev = ()
        if tc >= 2:
            wait_prev = ((s_sc.h_, chunk_s[tc - 2]) if False else (s_sc, chunk_s[tc - 2]),
                         (s_vc, xn_done_v[tc - 2]), (s_tc, chunk_t[tc - 2]))
        E_("sync", lambda e, tc=tc, bi=bi: e.dma_start(
            out=x_sb[bi][:, :], in_=x_in[tc * 128:(tc + 1) * 128, :]),
            inc=s_xl, inck=16, waits=wait_prev)
        XL = s_xl.n
        E_("vector", lambda e, bi=bi: e.tensor_reduce(
            out=mu_sb[:, :], in_=x_sb[bi][:, :], axis=AX.X, op=ALU.add),
            inc=s_vc, waits=((s_xl, XL), (s_sc, s_sc.n)))
        E_("scalar", lambda e, bi=bi: e.activation(
            out=sqd_sb[:, :], in_=x_sb[bi][:, 0:1], func=AF.Square,
            accum_out=ssq_sb[:, :]) if False else e.activation(
            out=xnf[bi][:, :], in_=x_sb[bi][:, :], func=AF.Square,
            accum_out=ssq_sb[:, :]),
            inc=s_sc, waits=((s_xl, XL), (s_vc, s_vc.n)))
        E_("vector", lambda e: e.tensor_scalar_mul(mu_sb[:, :], mu_sb[:, :], 1.0 / D),
            inc=s_vc, waits=(s_vc,))
        E_("vector", lambda e: e.tensor_tensor(
            out=var_sb[:, :], in0=mu_sb[:, :], in1=mu_sb[:, :], op=ALU.mult),
            inc=s_vc, waits=(s_vc,))
        E_("vector", lambda e: e.scalar_tensor_tensor(
            out=var_sb[:, :], in0=ssq_sb[:, :], scalar=1.0 / D, in1=var_sb[:, :],
            op0=ALU.mult, op1=ALU.subtract),
            inc=s_vc, waits=(s_vc, s_sc))
        E_("vector", lambda e: e.tensor_scalar_add(var_sb[:, :], var_sb[:, :], 1e-5),
            inc=s_vc, waits=(s_vc,))
        E_("scalar", lambda e: e.activation(out=var_sb[:, :], in_=var_sb[:, :],
                                            func=AF.Sqrt),
            inc=s_sc, waits=(s_vc,))
        E_("vector", lambda e: e.reciprocal(out=r_sb[:, :], in_=var_sb[:, :]),
            inc=s_vc, waits=(s_sc,))
        E_("vector", lambda e: e.tensor_tensor(
            out=nmr_sb[:, :], in0=mu_sb[:, :], in1=r_sb[:, :], op=ALU.mult),
            inc=s_vc, waits=(s_vc,))
        E_("vector", lambda e: e.tensor_scalar_mul(nmr_sb[:, :], nmr_sb[:, :], -1.0),
            inc=s_vc, waits=(s_vc,))
        V_STATS = s_vc.n
        E_("scalar", lambda e, bi=bi: e.activation(
            out=xnf[bi][:, :], in_=x_sb[bi][:, :], func=AF.Identity,
            bias=nmr_sb[:, :], scale=r_sb[:, :]),
            inc=s_sc, waits=((s_vc, V_STATS),))
        SC_XN = s_sc.n
        chunk_s.append(SC_XN)
        E_("vector", lambda e, bi=bi: e.tensor_copy(out=xn_tmb[bi][:, :], in_=xnf[bi][:, :]),
            inc=s_vc, waits=((s_sc, SC_XN), (s_xdp[bi], xd_done.get(tc - 2, 0)))),
        V_TMB = s_vc.n
        E_("sync", lambda e, tc=tc, bi=bi: e.dma_start(
            out=xn_dram.ap()[tc * 128:(tc + 1) * 128, :], in_=xn_tmb[bi][:, :]),
            inc=s_xdp[bi], inck=16, waits=((s_vc, V_TMB),))
        xd_done[tc] = s_xdp[bi].n
        for kc in range(KC):
            pt = [PA, PB][kc % 2]
            E_("tensor", lambda e, bi=bi, kc=kc, pt=pt: e.transpose(
                pt[:, 0:128], xnf[bi][:, kc * 128:(kc + 1) * 128], eyeF[:, :]),
                inc=s_tc, waits=((s_sc, SC_XN), (s_vc, s_vc.n)))
            T_NOW = s_tc.n
            E_("vector", lambda e, kc=kc, tc=tc, pt=pt: e.tensor_copy(
                out=xn_fm[:, kc, tc * 128:(tc + 1) * 128], in_=pt[:, 0:128]),
                inc=s_vc, waits=((s_tc, T_NOW),))
            E_("vector", lambda e, kc=kc, tc=tc, pt=pt, bi=bi: e.scalar_tensor_tensor(
                out=xnlo[bi][:, kc, :], in0=pt[:, 0:128], scalar=1.0,
                in1=xn_fm[:, kc, tc * 128:(tc + 1) * 128],
                op0=ALU.mult, op1=ALU.subtract),
                inc=s_vc, waits=(s_vc,))
        V_CHUNK = s_vc.n
        xn_done_v.append(V_CHUNK)
        for pas in range(4):
            for kc in range(KC):
                lhs = [wgh_sb, wgh_sb, wgl_sb, wgl_sb][pas]
                rhs_hi = pas in (0, 2)
                bank = (tc * 128 * 4) // 2048
                start = bank not in gate_started
                gate_started.add(bank)
                islast = ((tc % 4 == 3) or (tc == TC - 1)) and pas == 3 and kc == KC - 1
                E_("tensor", lambda e, tc=tc, kc=kc, lhs=lhs, rhs_hi=rhs_hi,
                   start=start, islast=islast, bi=bi: e.matmul(
                       ps_gate[:, tc * 128:(tc + 1) * 128],
                       lhs[:, kc, :],
                       xn_fm[:, kc, tc * 128:(tc + 1) * 128] if rhs_hi
                       else xnlo[bi][:, kc, :],
                       start=start, stop=islast),
                   inc=s_tc, waits=((s_vc, V_CHUNK), (s_vc, V_WGL)))
        chunk_t.append(s_tc.n)
    T_GATES = s_tc.n
    XD_ALL = tuple((s, s.n) for s in s_xdp)

    for c5 in range((S + 511) // 512):
        w = min(512, S - c5 * 512)
        slg = s_lgp[c5 % 2]
        E_("vector", lambda e, c5=c5, w=w: e.tensor_copy(
            out=lgev_sb2[c5 % 2][:, 0:w], in_=ps_gate[:, c5 * 512:c5 * 512 + w]),
            inc=s_vc, waits=((s_tc, T_GATES), (slg, slg.n)))
        V_LG = s_vc.n
        E_("sync", lambda e, c5=c5, w=w: e.dma_start(
            out=lg_dram.ap()[:, c5 * 512:c5 * 512 + w], in_=lgev_sb2[c5 % 2][:, 0:w]),
            inc=slg, inck=16, waits=((s_vc, V_LG),))
    E_("gpsimd", lambda e: e.collective_compute(
        "AllGather", ALU.bypass, replica_groups=[list(range(NC))],
        ins=[lg_dram.ap().opt()], outs=[ag_lg.ap().opt()]),
        inc=s_cc1, waits=tuple((s, s.n) for s in s_lgp))
    E_("gpsimd", lambda e: e.collective_compute(
        "AllGather", ALU.bypass, replica_groups=[list(range(NC))],
        ins=[xn_dram.ap().opt()], outs=[ag_xn.ap().opt()]),
        inc=s_cc2, waits=XD_ALL)

    regE.reset()
    wstr = [regE.bf(KC, 128) for _ in range(4)]
    qraw = regE.bf(S)
    crep = regE.bf(S)
    srep = regE.bf(S)
    ropeu = regE.bf(S)[0:32, :]
    ropev = regE.bf(S)[0:32, :]
    wv_half = regE.bf_at(0, KC, min(512, D))
    regC.reset()
    CWc = min(512, S)
    angc = regC.f32(CWc)[0:32, :]
    angr = regC.f32(CWc)[0:32, :]
    kint = regC.f32(CWc).bitcast(I32)[0:32, :]
    kf = regC.f32(CWc)[0:32, :]
    csb = regC.f32(CWc)[0:64, :]
    tlhf = regC.f32(CWc)[0:16, :]
    tlhb = regC.bf(CWc)[0:16, :]
    pos16 = regC.f32(CWc)[0:16, :]
    TWO_PI = float(2 * np.pi)
    E_("vector", lambda e: e.tensor_scalar_max(pa_sb[:, 0:1], ba_sb[:, :], 0.01),
        inc=s_vc, waits=((s_cl, CL),))
    E_("vector", lambda e: e.tensor_scalar(
        out=pa_sb[:, 1:2], in0=bp_sb[:, :], scalar1=0.01, scalar2=-1.0,
        op0=ALU.max, op1=ALU.mult), inc=s_vc, waits=(s_vc,))
    E_("vector", lambda e: e.tensor_scalar_mul(pa_sb[:, 2:3], pa_sb[:, 0:1], -1.0),
        inc=s_vc, waits=(s_vc,))
    E_("vector", lambda e: e.tensor_scalar(
        out=pa_sb[:, 3:4], in0=pa_sb[:, 0:1], scalar1=float(S - 1), scalar2=1.0,
        op0=ALU.mult, op1=ALU.add), inc=s_vc, waits=(s_vc,))
    V_PA = s_vc.n

    MAGIC = 8388608.0

    def range_reduce(dst0, shift):
        E_("vector", lambda e, shift=shift: e.tensor_scalar(
            out=angr, in0=angc, scalar1=1.0 / TWO_PI,
            scalar2=shift / TWO_PI, op0=ALU.mult, op1=ALU.add),
            inc=s_vc, waits=((s_vc, s_vc.n),))
        E_("vector", lambda e: e.tensor_scalar_add(angr, angr, MAGIC),
            inc=s_vc, waits=(s_vc,))
        E_("vector", lambda e: e.tensor_scalar_add(kf, angr, -MAGIC),
            inc=s_vc, waits=(s_vc,))
        E_("vector", lambda e, dst0=dst0: e.scalar_tensor_tensor(
            out=csb[dst0:dst0 + 32, :], in0=kf, scalar=-TWO_PI,
            in1=angc, op0=ALU.mult, op1=ALU.add),
            inc=s_vc, waits=(s_vc,))
        if shift:
            E_("vector", lambda e, dst0=dst0, shift=shift: e.tensor_scalar_add(
                csb[dst0:dst0 + 32, :], csb[dst0:dst0 + 32, :], shift),
                inc=s_vc, waits=(s_vc,))

    for cb in range(S // CWc):
        c0 = cb * CWc
        E_("sync", lambda e, c0=c0: e.dma_start(
            out=pos16, in_=AP(posc.tensor, c0, [[0, 16], [1, CWc]])),
            inc=s_cl, inck=16, waits=((s_sc, s_sc.n), (s_tc, T_GATES)))
        CL16 = s_cl.n
        E_("tensor", lambda e, c0=c0: e.matmul(
            PC[0:32, 0:CWc], fr_sb[:, :], pos_sb[:, c0:c0 + CWc],
            start=True, stop=True),
            inc=s_tc, waits=((s_cl, CL), (s_tc, T_GATES), (s_vc, s_vc.n)))
        T_ANG = s_tc.n
        E_("vector", lambda e: e.tensor_copy(out=angc, in_=PC[0:32, 0:CWc]),
            inc=s_vc, waits=((s_tc, T_ANG),))
        range_reduce(0, 0.0)
        range_reduce(32, float(np.pi / 2))
        V_CS = s_vc.n
        E_("scalar", lambda e: e.activation(out=csb, in_=csb, func=AF.Sin),
            inc=s_sc, waits=((s_vc, V_CS),))
        SC_CS = s_sc.n
        for q4 in range(4):
            E_("vector", lambda e, q4=q4, c0=c0: e.tensor_copy(
                out=crep[q4 * 32:(q4 + 1) * 32, c0:c0 + CWc], in_=csb[32:64, :]),
                inc=s_vc, waits=((s_sc, SC_CS),))
            E_("vector", lambda e, q4=q4, c0=c0: e.tensor_copy(
                out=srep[q4 * 32:(q4 + 1) * 32, c0:c0 + CWc], in_=csb[0:32, :]),
                inc=s_vc, waits=((s_sc, SC_CS),))
        E_("scalar", lambda e: e.activation(
            out=tlhf[0:H, :], in_=pos16[0:H, :], func=AF.Ln,
            bias=1.0, scale=pa_sb[0:H, 0:1]),
            inc=s_sc, waits=((s_vc, V_PA), (s_cl, CL16), (s_td, s_td.n)))
        E_("scalar", lambda e: e.mul(tlhf[0:H, :], tlhf[0:H, :], pa_sb[0:H, 1:2]),
            inc=s_sc, waits=(s_sc,))
        E_("scalar", lambda e: e.activation(out=tlhf[0:H, :], in_=tlhf[0:H, :],
                                            func=AF.Exp),
            inc=s_sc, waits=(s_sc,))
        E_("vector", lambda e: e.tensor_copy(out=tlhb[0:H, :], in_=tlhf[0:H, :]),
            inc=s_vc, waits=((s_sc, s_sc.n),))
        V_TB = s_vc.n
        E_("sync", lambda e, c0=c0: e.dma_start(
            out=tall_dram.ap()[:, S - 1 + c0:S - 1 + c0 + CWc], in_=tlhb[0:H, :]),
            inc=s_td, inck=16, waits=((s_vc, V_TB),))
        cwd = min(CWc, (S - 1) - c0)
        E_("scalar", lambda e: e.activation(
            out=tlhf[0:H, :], in_=pos16[0:H, :], func=AF.Ln,
            bias=pa_sb[0:H, 3:4], scale=pa_sb[0:H, 2:3]),
            inc=s_sc, waits=((s_vc, V_TB), (s_td, s_td.n)))
        E_("scalar", lambda e: e.mul(tlhf[0:H, :], tlhf[0:H, :], pa_sb[0:H, 1:2]),
            inc=s_sc, waits=(s_sc,))
        E_("scalar", lambda e: e.activation(out=tlhf[0:H, :], in_=tlhf[0:H, :],
                                            func=AF.Exp),
            inc=s_sc, waits=(s_sc,))
        E_("vector", lambda e: e.tensor_copy(out=tlhb[0:H, :], in_=tlhf[0:H, :]),
            inc=s_vc, waits=((s_sc, s_sc.n),))
        V_TB2 = s_vc.n
        if cwd > 0:
            E_("sync", lambda e, c0=c0, cwd=cwd: e.dma_start(
                out=tall_dram.ap()[:, c0:c0 + cwd], in_=tlhb[0:H, 0:cwd]),
                inc=s_td, inck=16, waits=((s_vc, V_TB2),))
    V_CSREP = s_vc.n
    TD_ALL = s_td.n

    regB.reset()
    q_fm = regB.bf(OC, S)
    regC.reset()
    k_fm = regC.bf(OC, S)

    wload_v = {}
    LG_ALL = tuple((s, s.n) for s in s_lgp)
    for wi, (w_in, dst_fm) in enumerate(((wq_in, q_fm), (wk_in, k_fm))):
        region_guard = LG_ALL if wi == 0 else ((s_td, TD_ALL),)
        for oc in range(OC):
            slot = (wi * OC + oc) % 4
            sbuf_w = wstr[slot]
            swl = s_wstr[slot]
            E_("gpsimd", lambda e, w_in=w_in, oc=oc, sbuf_w=sbuf_w: e.dma_start(
                out=sbuf_w[:, :, :], in_=w_in[:, oc, :, :]),
                inc=swl, inck=16,
                waits=((s_tc, wload_v.get((wi * OC + oc) - 4, 0)),))
            WL = (swl, swl.n)
            for t4 in range(NQC):
                pt = [PA, PB][t4 % 2]
                for kc in range(KC):
                    E_("tensor", lambda e, oc=oc, t4=t4, kc=kc, pt=pt, sbuf_w=sbuf_w: e.matmul(
                        pt[:, 0:QCW], sbuf_w[:, kc, :],
                        xn_fm[:, kc, t4 * QCW:(t4 + 1) * QCW],
                        start=(kc == 0), stop=(kc == KC - 1)),
                        inc=s_tc,
                        waits=((s_vc, xn_done_v[-1]), WL, (s_vc, s_vc.n)))
                T_MM = s_tc.n
                E_("vector", lambda e, t4=t4, pt=pt: e.tensor_copy(
                    out=qraw[:, t4 * QCW:(t4 + 1) * QCW], in_=pt[:, 0:QCW]),
                    inc=s_vc, waits=((s_tc, T_MM),))
            wload_v[wi * OC + oc] = s_tc.n
            V_QR = s_vc.n
            for q2 in range(2):
                b0 = q2 * 64
                t1s = slice(b0, b0 + 32)
                t2s = slice(b0 + 32, b0 + 64)
                E_("vector", lambda e, t1s=t1s: e.tensor_tensor(
                    out=ropeu, in0=qraw[t1s, :], in1=crep[t1s, :], op=ALU.mult),
                    inc=s_vc, waits=((s_vc, V_QR), (s_vc, V_CSREP), (s_vc, s_vc.n))
                    + region_guard)
                E_("vector", lambda e, t2s=t2s: e.tensor_tensor(
                    out=ropev, in0=qraw[t2s, :], in1=srep[t2s, :], op=ALU.mult),
                    inc=s_vc, waits=(s_vc,))
                E_("vector", lambda e, oc=oc, t1s=t1s, dst_fm=dst_fm: e.tensor_tensor(
                    out=dst_fm[t1s, oc, :], in0=ropeu, in1=ropev, op=ALU.subtract),
                    inc=s_vc, waits=(s_vc,))
                E_("vector", lambda e, t1s=t1s: e.tensor_tensor(
                    out=ropeu, in0=qraw[t1s, :], in1=srep[t1s, :], op=ALU.mult),
                    inc=s_vc, waits=(s_vc,))
                E_("vector", lambda e, t2s=t2s: e.tensor_tensor(
                    out=ropev, in0=qraw[t2s, :], in1=crep[t2s, :], op=ALU.mult),
                    inc=s_vc, waits=(s_vc,))
                E_("vector", lambda e, oc=oc, t2s=t2s, dst_fm=dst_fm: e.tensor_tensor(
                    out=dst_fm[t2s, oc, :], in0=ropeu, in1=ropev, op=ALU.add),
                    inc=s_vc, waits=(s_vc,))
    V_QK = s_vc.n
    E_("vector", lambda e: e.memset(
        v_buf[:, :, :].rearrange("p t (h u) -> p t h u", u=65)[:, :, :, 64:65], 1.0),
        inc=s_vc, waits=())
    V_VONES = s_vc.n
    VCW = min(512, D)
    for vc in range(D // VCW):
        E_("gpsimd", lambda e, vc=vc: e.dma_start(
            out=wv_half[:, :, :], in_=wv_in[:, :, vc * VCW:(vc + 1) * VCW]),
            inc=s_wl, inck=16, waits=((s_tc, s_tc.n),))
        WL_V = s_wl.n
        for tc in range(TC):
            pt = [PA, PB][tc % 2]
            for kc in range(KC):
                E_("tensor", lambda e, tc=tc, kc=kc, pt=pt: e.matmul(
                    pt[:, 0:VCW], xn_fm[:, kc, tc * 128:(tc + 1) * 128],
                    wv_half[:, kc, :],
                    start=(kc == 0), stop=(kc == KC - 1)),
                    inc=s_tc, waits=((s_wl, WL_V), (s_vc, s_vc.n)))
            T_MM = s_tc.n
            nh = VCW // 64
            E_("vector", lambda e, tc=tc, vc=vc, pt=pt, nh=nh: e.tensor_copy(
                out=v_buf[:, tc, :].rearrange("p (h u) -> p h u", u=65)
                [:, vc * nh:(vc + 1) * nh, 0:64],
                in_=pt[:, 0:VCW].rearrange("p (h u) -> p h u", u=64)),
                inc=s_vc, waits=((s_tc, T_MM), (s_vc, V_VONES)))
    V_V = s_vc.n
    T_P3 = s_tc.n

    regA.reset()
    L_sb = regA.f32(S)
    ltm = regA.f32(NJ * E)
    mask_sb = regA.bf(NJ * E)
    tmpA = regA.f32(NJ * E)
    rec_wait_cc1 = (s_cc1, 1)
    E_("sync", lambda e: e.dma_start(out=L_sb[0:NC * E, 0:S], in_=ag_lg.ap()[:, :]),
        inc=s_gl, inck=16, waits=(rec_wait_cc1, (s_vc, V_QK), (s_tc, T_P3)))
    GL = s_gl.n
    for m in range((NC * E) // 32):
        for cc in range(TC):
            pt = [PA, PB][(m * TC + cc) % 2]
            E_("tensor", lambda e, m=m, cc=cc, pt=pt: e.transpose(
                pt[:, 0:32], L_sb[m * 32:(m + 1) * 32, cc * 128:(cc + 1) * 128],
                eyeF[m * 32:(m + 1) * 32, m * 32:(m + 1) * 32],
                tile_position=(m * 32, 0)), inc=s_tc,
                waits=((s_gl, GL), (s_vc, s_vc.n)))
            T_NOW = s_tc.n
            for hh in range(2):
                j = (2 * m + hh) * TC + cc
                E_("vector", lambda e, pt=pt, hh=hh, j=j: e.tensor_copy(
                    out=ltm[:, j * E:(j + 1) * E], in_=pt[:, hh * 16:hh * 16 + E]),
                    inc=s_vc, waits=((s_tc, T_NOW),))
    V_LTM = s_vc.n
    ltm3 = ltm.rearrange("p (j e) -> p j e", e=E)
    tmpA3 = tmpA.rearrange("p (j e) -> p j e", e=E)
    mask3 = mask_sb.rearrange("p (j e) -> p j e", e=E)
    E_("vector", lambda e: e.tensor_reduce(out=m_sb[:, :], in_=ltm3, axis=AX.X,
                                           op=ALU.max),
        inc=s_vc, waits=((s_vc, V_LTM),))
    E_("vector", lambda e: e.tensor_tensor(
        out=tmpA3, in0=ltm3, in1=m_sb[:, :].unsqueeze(2).to_broadcast([128, NJ, E]),
        op=ALU.subtract), inc=s_vc, waits=(s_vc,))
    V_SUB = s_vc.n
    E_("scalar", lambda e: e.activation(out=tmpA, in_=tmpA, func=AF.Exp),
        inc=s_sc, waits=((s_vc, V_SUB),))
    SC_EXPG = s_sc.n
    E_("vector", lambda e: e.tensor_reduce(out=s_sum[:, :], in_=tmpA3, axis=AX.X,
                                           op=ALU.add),
        inc=s_vc, waits=((s_sc, SC_EXPG),))
    E_("vector", lambda e: e.reciprocal(out=gsc_sb[:, :], in_=s_sum[:, :]),
        inc=s_vc, waits=(s_vc,))
    V_GSC = s_vc.n
    E_("vector", lambda e: e.tensor_tensor(
        out=mask3, in0=ltm3, in1=m_sb[:, :].unsqueeze(2).to_broadcast([128, NJ, E]),
        op=ALU.is_ge), inc=s_vc, waits=(s_vc,))
    V_MASK = s_vc.n
    E_("vector", lambda e: e.tensor_tensor(
        out=tmpA3, in0=mask3, in1=rev_sb[:, :].unsqueeze(1).to_broadcast([128, NJ, E]),
        op=ALU.mult), inc=s_vc, waits=(s_vc,))
    E_("vector", lambda e: e.tensor_reduce(out=tmpB[:, :], in_=tmpA3, axis=AX.X,
                                           op=ALU.max),
        inc=s_vc, waits=(s_vc,))
    E_("vector", lambda e: e.tensor_scalar(
        out=idx_g[:, :], in0=tmpB[:, :], scalar1=-1.0, scalar2=float(E - 1),
        op0=ALU.mult, op1=ALU.add), inc=s_vc, waits=(s_vc,))
    V_IDX = s_vc.n
    ECW = min(512, NJ * E)
    cum_started = []
    for b in range((NJ * E) // ECW):
        E_("tensor", lambda e, b=b: e.matmul(
            PBIG[:, b * ECW:(b + 1) * ECW], trilI[:, :],
            mask_sb[:, b * ECW:(b + 1) * ECW], start=True, stop=False),
            inc=s_tc, waits=((s_vc, V_MASK), (s_cl, CL), (s_vc, s_vc.n)))
    T_CUM1 = s_tc.n
    for b in range((NJ * E + 511) // 512):
        w = min(512, NJ * E - b * 512)
        E_("tensor", lambda e, b=b, w=w: e.matmul(
            PA[0:1, 0:w], onesC[:, :], mask_sb[:, b * 512:b * 512 + w],
            start=True, stop=True),
            inc=s_tc, waits=((s_vc, V_MASK), (s_cl, CL), (s_vc, s_vc.n)))
        T_TOTB = s_tc.n
        E_("vector", lambda e, b=b, w=w: e.tensor_copy(
            out=row8k[:, b * 512:b * 512 + w], in_=PA[0:1, 0:w]),
            inc=s_vc, waits=((s_tc, T_TOTB),))
    V_TOT = s_vc.n
    rd1 = rdpool.next()
    E_("gpsimd", lambda e: e.dma_start(
        out=tot_dram.ap().unsqueeze(0),
        in_=row8k[:, 0:NJ * E].rearrange("q (j e) -> q j e", e=E)),
        inc=rd1, inck=16, waits=((s_vc, V_TOT),))
    rd2 = rdpool.next()
    E_("gpsimd", lambda e: e.dma_start(out=tots_sb[:, :], in_=tot_dram.ap()[:, :]),
        inc=rd2, inck=16, waits=(rd1,))
    RD_TOTS = (rd2, rd2.n)
    E_("tensor", lambda e: e.matmul(PA[0:NJ, 0:E], trilS[0:NJ, 0:NJ], tots_sb[0:NJ, :],
                                    start=True, stop=True),
        inc=s_tc, waits=(RD_TOTS, (s_vc, s_vc.n)))
    T_OFFS = s_tc.n
    E_("vector", lambda e: e.tensor_copy(out=offs_sb[:, :], in_=PA[0:NJ, 0:E]),
        inc=s_vc, waits=((s_tc, T_OFFS),))
    V_OFFS = s_vc.n
    rd3 = rdpool.next()
    E_("gpsimd", lambda e: e.dma_start(out=off_dram.ap()[:, :], in_=offs_sb[:, :]),
        inc=rd3, inck=16, waits=((s_vc, V_OFFS),))
    rd4 = rdpool.next()
    E_("gpsimd", lambda e: e.dma_start(
        out=row8k[:, 0:NJ * E].rearrange("q (j e) -> q j e", e=E),
        in_=off_dram.ap().unsqueeze(0)),
        inc=rd4, inck=16, waits=(rd3, (s_vc, s_vc.n)))
    RD_OFFR = (rd4, rd4.n)
    for b in range((NJ * E) // ECW):
        E_("tensor", lambda e, b=b: e.matmul(
            PBIG[:, b * ECW:(b + 1) * ECW], onesR[:, :],
            row8k[:, b * ECW:b * ECW + ECW], start=False, stop=True),
            inc=s_tc, waits=(RD_OFFR,))
    T_CUM2 = s_tc.n
    E_("vector", lambda e: e.tensor_tensor(
        out=tmpA, in0=PBIG[:, 0:NJ * E], in1=mask_sb[:, :], op=ALU.mult),
        inc=s_vc, waits=((s_tc, T_CUM2),))
    E_("vector", lambda e: e.tensor_reduce(out=loc_g[:, :], in_=tmpA3, axis=AX.X,
                                           op=ALU.add),
        inc=s_vc, waits=(s_vc,))
    E_("vector", lambda e: e.tensor_scalar_add(loc_g[:, :], loc_g[:, :], -1.0),
        inc=s_vc, waits=(s_vc,))
    V_LOC = s_vc.n
    E_("vector", lambda e: e.tensor_scalar(
        out=val_g[:, :], in0=loc_g[:, :], scalar1=float(C - 1), scalar2=None,
        op0=ALU.is_le), inc=s_vc, waits=(s_vc,))
    E_("vector", lambda e: e.tensor_tensor(out=w_g[:, :], in0=gsc_sb[:, :],
                                           in1=val_g[:, :], op=ALU.mult),
        inc=s_vc, waits=((s_vc, s_vc.n), (s_vc, V_GSC)))
    V_W = s_vc.n
    E_("vector", lambda e: e.tensor_scalar(
        out=tmpB[:, :], in0=idx_g[:, :], scalar1=twoc_sb[:, :], scalar2=None,
        op0=ALU.subtract), inc=s_vc, waits=((s_vc, V_IDX), (s_cl, CL)))
    E_("vector", lambda e: e.tensor_scalar(
        out=dD_g[:, :], in0=tmpB[:, :], scalar1=0.0, scalar2=None, op0=ALU.is_ge),
        inc=s_vc, waits=(s_vc,))
    E_("vector", lambda e: e.tensor_scalar(
        out=tmpC[:, :], in0=tmpB[:, :], scalar1=1.0, scalar2=None, op0=ALU.is_le),
        inc=s_vc, waits=(s_vc,))
    E_("vector", lambda e: e.tensor_tensor(out=dD_g[:, :], in0=dD_g[:, :],
                                           in1=tmpC[:, :], op=ALU.mult),
        inc=s_vc, waits=(s_vc,))
    E_("vector", lambda e: e.tensor_tensor(out=dD_g[:, :], in0=dD_g[:, :],
                                           in1=val_g[:, :], op=ALU.mult),
        inc=s_vc, waits=(s_vc,))
    E_("vector", lambda e: e.tensor_scalar(
        out=tmpB[:, :], in0=tmpB[:, :], scalar1=float(C), scalar2=None, op0=ALU.mult),
        inc=s_vc, waits=(s_vc,))
    E_("vector", lambda e: e.tensor_tensor(out=tmpB[:, :], in0=tmpB[:, :],
                                           in1=loc_g[:, :], op=ALU.add),
        inc=s_vc, waits=((s_vc, s_vc.n), (s_vc, V_LOC)))
    E_("vector", lambda e: e.tensor_scalar_add(tmpB[:, :], tmpB[:, :], -float(NS)),
        inc=s_vc, waits=(s_vc,))
    E_("vector", lambda e: e.tensor_tensor(out=tmpB[:, :], in0=tmpB[:, :],
                                           in1=dD_g[:, :], op=ALU.mult),
        inc=s_vc, waits=(s_vc,))
    E_("vector", lambda e: e.tensor_scalar_add(dD_g[:, :], tmpB[:, :], float(NS)),
        inc=s_vc, waits=(s_vc,))
    V_D = s_vc.n
    rd5 = rdpool.next()
    E_("gpsimd", lambda e: e.dma_start(out=dT_dram.ap()[:, :], in_=dD_g[:, :]),
        inc=rd5, inck=16, waits=((s_vc, V_D),))
    RD_DT = (rd5, rd5.n)
    rd5b = rdpool.next()
    E_("gpsimd", lambda e: e.dma_start(
        out=didx_sb[:, :].rearrange("q (t j) -> q t j", j=NJ),
        in_=dT_dram.ap().rearrange("(t q) j -> q t j", q=16)),
        inc=rd5b, inck=16, waits=(RD_DT,))
    RD_DIDX = (rd5b, rd5b.n)
    V_DIDX = s_vc.n
    E_("gpsimd", lambda e: e.memset(table_sb[:, :], 0.0), inc=s_ms,
       waits=gp_barrier())
    NCH = TALL // 1024
    for k in range(NCH):
        bi = k % 2
        E_("gpsimd", lambda e, k=k, bi=bi: e.dma_start(
            out=iop_sb[bi][:, :], in_=iopc[:, k * 2048:(k + 1) * 2048]),
            inc=s_iop, inck=16,
            waits=((s_gp, max(0, s_gp.n - 1)), (s_ms, s_ms.n)))
        IOP = s_iop.n
        E_("gpsimd", lambda e, k=k, bi=bi: e.scatter_add(
            table_sb[:, :].rearrange("p (n d) -> p n d", d=2),
            didx_sb[:, k * 64:(k + 1) * 64],
            iop_sb[bi][:, :].rearrange("p (n d) -> p n d", d=2),
            channels=16, num_elems=NS + 16, d=2, num_idxs=1024),
            inc=s_gp, waits=gp_barrier() + ((s_iop, IOP), (s_ms, s_ms.n),
                                            RD_DIDX, (s_cl, CL)))
    GP_SCAT = s_gp.n
    tab3 = table_sb[:, :].rearrange("p (n d) -> p n d", d=2)
    E_("vector", lambda e: e.scalar_tensor_tensor(
        out=row8k[:, 0:NS], in0=tab3[0:1, 0:NS, 0], scalar=128.0,
        in1=tab3[0:1, 0:NS, 1], op0=ALU.mult, op1=ALU.add),
        inc=s_vc, waits=((s_gp, GP_SCAT),))
    V_TOK = s_vc.n
    rd6 = rdpool.next()
    E_("gpsimd", lambda e: e.dma_start(
        out=tok_dram.ap().unsqueeze(0),
        in_=row8k[:, 0:NS].rearrange("q (c p) -> q c p", p=128)),
        inc=rd6, inck=16, waits=((s_vc, V_TOK),))
    rd7 = rdpool.next()
    E_("gpsimd", lambda e: e.dma_start(out=tokA_sb[:, :], in_=tok_dram.ap()[:, :]),
        inc=rd7, inck=16, waits=(rd6,))
    RD_TOK = (rd7, rd7.n)
    E_("tensor", lambda e: e.transpose(PA[:, 0:NSC],
                                       tokA_sb[0:NSC, :], eyeF[0:NSC, 0:NSC]),
        inc=s_tc, waits=(RD_TOK, (s_vc, s_vc.n)))
    T_TOK = s_tc.n
    E_("vector", lambda e: e.tensor_copy(out=dgi_sb[:, :], in_=PA[:, 0:NSC]),
        inc=s_vc, waits=((s_tc, T_TOK),))
    V_DGI = s_vc.n
    E_("vector", lambda e: e.tensor_scalar_min(tmpB[:, :], loc_g[:, :], float(C - 1)),
        inc=s_vc, waits=((s_vc, V_LOC), (s_vc, s_vc.n)))
    E_("vector", lambda e: e.scalar_tensor_tensor(
        out=rR_g[:, :], in0=idx_g[:, :], scalar=float(C), in1=tmpB[:, :],
        op0=ALU.mult, op1=ALU.add), inc=s_vc, waits=((s_vc, s_vc.n), (s_vc, V_IDX)))
    V_R = s_vc.n

    def extract_own(grid_sb, dst_sb, wv):
        E_("tensor", lambda e, grid_sb=grid_sb: e.transpose(
            PA[0:NJ, 0:128], grid_sb[:, 0:NJ], eyeF[:, :]),
            inc=s_tc, waits=((s_vc, wv), (s_vc, s_vc.n)))
        T1 = s_tc.n
        E_("vector", lambda e: e.tensor_copy(out=gridT_sb[0:NJ, :], in_=PA[0:NJ, 0:128]),
            inc=s_vc, waits=((s_tc, T1),))
        V1 = s_vc.n
        E_("tensor", lambda e: e.matmul(PB[0:TC, 0:128], sown_sb[0:NJ, :],
                                        gridT_sb[0:NJ, :], start=True, stop=True),
            inc=s_tc, waits=((s_vc, V1), (s_cl, CL)))
        T2 = s_tc.n
        E_("vector", lambda e: e.tensor_copy(out=extr_sb[:, :], in_=PB[0:TC, 0:128]),
            inc=s_vc, waits=((s_tc, T2),))
        V2 = s_vc.n
        E_("tensor", lambda e: e.transpose(PA[:, 0:TC], extr_sb[:, :],
                                           eyeF[0:TC, 0:TC]),
            inc=s_tc, waits=((s_vc, V2),))
        T3 = s_tc.n
        E_("vector", lambda e, dst_sb=dst_sb: e.tensor_copy(out=dst_sb[:, :],
                                                            in_=PA[:, 0:TC]),
            inc=s_vc, waits=((s_tc, T3),))
        return s_vc.n
    V_RIDX = extract_own(rR_g, ridx_sb, V_R)
    V_WOWN = extract_own(w_g, wown_sb, V_W)

    if "qk" in debug:
        d1 = dbg_out("qfm", (128, OC * S), BF16)
        d2 = dbg_out("kfm", (128, OC * S), BF16)
        E_("sync", lambda e, d1=d1: e.dma_start(
            out=d1[:, :], in_=q_fm[:, :, :].rearrange("p a b -> p (a b)")),
            inc=s_outp[0], inck=16, waits=((s_vc, V_QK),))
        E_("sync", lambda e, d2=d2: e.dma_start(
            out=d2[:, :], in_=k_fm[:, :, :].rearrange("p a b -> p (a b)")),
            inc=s_outp[0], inck=16, waits=((s_vc, V_QK),))
    if "xn" in debug:
        d = dbg_out("xn", (S, D), BF16)
        E_("sync", lambda e, d=d: e.dma_start(out=d[:, :], in_=xn_dram.ap()[:, :]),
            inc=s_outp[0], inck=16, waits=XD_ALL)
    if "lg" in debug:
        d = dbg_out("lg", (NC * E, S))
        E_("sync", lambda e, d=d: e.dma_start(out=d[:, :], in_=ag_lg.ap()[:, :]),
            inc=s_outp[0], inck=16, waits=((s_cc1, 1),))
    if "idx" in debug:
        d = dbg_out("idx", (128, NJ))
        E_("sync", lambda e, d=d: e.dma_start(out=d[:, :], in_=idx_g[:, :]),
            inc=s_outp[0], inck=16, waits=((s_vc, V_IDX),))
    if "loc" in debug:
        d = dbg_out("loc", (128, NJ))
        E_("sync", lambda e, d=d: e.dma_start(out=d[:, :], in_=loc_g[:, :]),
            inc=s_outp[0], inck=16, waits=((s_vc, V_LOC),))
    if "w" in debug:
        d = dbg_out("w", (128, NJ))
        E_("sync", lambda e, d=d: e.dma_start(out=d[:, :], in_=w_g[:, :]),
            inc=s_outp[0], inck=16, waits=((s_vc, V_W),))
    if "dgi" in debug:
        d = dbg_out("dgi", (128, NSC), I32)
        E_("sync", lambda e, d=d: e.dma_start(out=d[:, :], in_=dgi_sb[:, :]),
            inc=s_outp[0], inck=16, waits=((s_vc, V_DGI),))

    regA.reset()
    tbuf = [regA.bf(GW) for _ in range(2)]
    p0_sb = [regA.bf(2 * QCW) for _ in range(2)]
    pt_sb = [regA.bf(2, QCW) for _ in range(2)]
    linv_sb = regA.f32(QCW)[0:1, :]
    lb_sb = regA.bf(QCW)[0:64, :]
    regE.reset()
    attn_fm = regE.bf(OC, S)

    V_PRE4 = s_vc.n
    T_PRE4 = s_tc.n
    tb_v = {-2: V_PRE4, -1: V_PRE4}
    av_done = {}
    exp_s = {}
    mul_v = {}
    for h in range(H):
        oc = h // 2
        rb = (h % 2) * 64
        ti = h % 2
        E_("gpsimd", lambda e, h=h, ti=ti: e.dma_start(
            out=tbuf[ti],
            in_=AP(tall_dram.ap().tensor, h * TROW, [[1, 128], [1, GW]])),
            inc=s_tb[ti], inck=16, waits=((s_td, TD_ALL), (s_vc, tb_v.get(h - 2, 0)),
                                          (s_tc, T_PRE4), (s_gp, s_gp.n)))
        GP_TB = (s_tb[ti], s_tb[ti].n)
        for qc in range(NQC):
            q0 = qc * QCW
            def issue_av(kcp):
                ri = kcp % 2
                for half in range(2):
                    kc = kcp * 2 + half
                    E_("tensor", lambda e, h=h, kc=kc, ri=ri, half=half: e.matmul(
                        PA[0:65, 0:QCW], v_buf[:, kc, h * 65:(h + 1) * 65],
                        pt_sb[ri][:, half, :],
                        start=(kc == 0), stop=(kc == TC - 1)),
                        inc=s_tc,
                        waits=((s_vc, mul_v[(h, qc, kcp)]), (s_vc, V_V)))
                av_done[(h, qc, kcp)] = s_tc.n

            for kcp in range(NKP):
                ri = kcp % 2
                scps = PBIG[:, ri * 1024:ri * 1024 + 2 * QCW]
                for half in range(2):
                    kc = kcp * 2 + half
                    E_("tensor", lambda e, oc=oc, rb=rb, kc=kc, half=half, scps=scps, q0=q0: e.matmul(
                        scps[:, half * QCW:(half + 1) * QCW],
                        k_fm[rb:rb + 64, oc, kc * 128:(kc + 1) * 128],
                        q_fm[rb:rb + 64, oc, q0:q0 + QCW],
                        start=True, stop=True),
                        inc=s_tc,
                        waits=((s_vc, V_PRE4),
                               (s_sc, exp_s.get((h, qc, kcp - 2), 0))))
                T_SC = s_tc.n
                E_("scalar", lambda e, ri=ri, scps=scps: e.activation(
                    out=p0_sb[ri], in_=scps, func=AF.Exp, scale=SCALE),
                    inc=s_sc, waits=((s_tc, T_SC),
                                     (s_vc, max(mul_v.get((h, qc, kcp - 2), 0), V_PRE4))))
                exp_s[(h, qc, kcp)] = s_sc.n
                for half in range(2):
                    kc = kcp * 2 + half
                    g0 = (S - 1) + kc * 128 - q0
                    FSA = regA.t.shape[1]
                    rev1 = AP(regA.t, tbuf[ti].offset + g0, [[FSA, 128], [-1, QCW]])
                    E_("vector", lambda e, ri=ri, half=half, rev1=rev1: e.tensor_tensor(
                        out=pt_sb[ri][:, half, :],
                        in0=p0_sb[ri][:, half * QCW:(half + 1) * QCW],
                        in1=rev1, op=ALU.mult),
                        inc=s_vc,
                        waits=((s_sc, exp_s[(h, qc, kcp)]), GP_TB,
                               (s_tc, av_done.get((h, qc, kcp - 2), 0))))
                mul_v[(h, qc, kcp)] = s_vc.n
                if "pt" in debug and h == 9 and qc == 0:
                    if "pt" not in dbg:
                        dbg_out("pt", (S, QCW), BF16)
                        dbg_out("p0", (S, QCW), BF16)
                    dpt = dbg["pt"]
                    dp0 = dbg["p0"]
                    E_("sync", lambda e, kcp=kcp, ri=ri, dpt=dpt: e.dma_start(
                        out=dpt[kcp * 256:(kcp + 1) * 256, :].rearrange(
                            "(a p) b -> p a b", a=2),
                        in_=pt_sb[ri][:, :, :]),
                        inc=s_outp[0], inck=16, waits=((s_vc, s_vc.n),))
                    E_("sync", lambda e, kcp=kcp, ri=ri, dp0=dp0: e.dma_start(
                        out=dp0[kcp * 256:(kcp + 1) * 256, :].rearrange(
                            "(a p) b -> p a b", a=2),
                        in_=p0_sb[ri][:, :].rearrange("p (a b) -> p a b", a=2)),
                        inc=s_outp[0], inck=16, waits=((s_vc, s_vc.n),))
                if kcp >= 1:
                    issue_av(kcp - 1)
            issue_av(NKP - 1)
            T_AV = s_tc.n
            E_("vector", lambda e: e.reciprocal(out=linv_sb[:, :], in_=PA[64:65, 0:QCW]),
                inc=s_vc, waits=((s_tc, T_AV),))
            V_LI = s_vc.n
            E_("tensor", lambda e: e.matmul(PB[0:64, 0:QCW], onesR[:, 0:64],
                                            linv_sb[:, :], start=True, stop=True),
                inc=s_tc, waits=((s_vc, V_LI),))
            T_LB = s_tc.n
            E_("vector", lambda e: e.tensor_copy(out=lb_sb[:, 0:QCW], in_=PB[0:64, 0:QCW]),
                inc=s_vc, waits=((s_tc, T_LB),))
            V_LBE = s_vc.n
            E_("vector", lambda e, rb=rb, oc=oc, q0=q0: e.tensor_tensor(
                out=attn_fm[rb:rb + 64, oc, q0:q0 + QCW], in0=PA[0:64, 0:QCW],
                in1=lb_sb[:, 0:QCW], op=ALU.mult),
                inc=s_vc, waits=((s_vc, V_LBE),))
            mul_v[(h, qc, NKP - 1)] = s_vc.n
        tb_v[h] = s_vc.n
    V_ATTN = s_vc.n
    if "attn" in debug:
        d = dbg_out("attn", (128, OC * S), BF16)
        E_("sync", lambda e, d=d: e.dma_start(
            out=d[:, :], in_=attn_fm.rearrange("p a b -> p (a b)")),
            inc=s_outp[0], inck=16, waits=((s_vc, V_ATTN),))

    G = G5
    NG = FC // G
    regB.reset()
    w1g = [regB.bf(G, KC, 128) for _ in range(2)]
    w2g = [regB.bf(G, D) for _ in range(2)]
    regC.reset()
    y2acc = regC.f32(C // 128, D)
    regF.reset()
    dispE = regF.bf(KC, C)
    xg_sb = [regF.bf(D) for _ in range(2)]
    h_sbs = [sb(f"h_sbs{i}", (128, min(256, C)), BF16) for i in range(2)]
    y2e_sb = [sb(f"y2e_sb{i}", (128, D), BF16) for i in range(2)]
    TGW = min(256, C)
    NTG = C // TGW
    NT2 = TGW // 128
    OW = min(512, D)

    w1v = {}
    grp_t = {}
    T_P4B = s_tc.n

    def emit_wload(gidx):
        e_i, g = gidx // NG, gidx % NG
        wb = gidx % 2
        sw = s_w1p[wb]
        E_("gpsimd", lambda e, e_i=e_i, g=g, wb=wb: e.dma_start(
            out=w1g[wb][:, :, :, :], in_=w1_in[e_i][:, g * G:(g + 1) * G, :, :]),
            inc=sw, inck=16, waits=((s_tc, max(grp_t.get(gidx - 2, 0), T_P4B)),))
        E_("gpsimd", lambda e, e_i=e_i, g=g, wb=wb: e.dma_start(
            out=w2g[wb][:, :, :], in_=w2_in[e_i][:, g * G:(g + 1) * G, :]),
            inc=sw, inck=16, waits=((s_tc, max(grp_t.get(gidx - 2, 0), T_P4B)),))
        w1v[gidx] = (sw, sw.n)

    emit_wload(0)
    emit_wload(1)
    ig_prev = {}
    for e_i in range(EPC):
        for c2 in range(C // 128):
            cglob = (e_i * C) // 128 + c2
            xgi = cglob % 2
            sig = s_igp[xgi]
            E_("gpsimd", lambda e, cglob=cglob, xgi=xgi: e.indirect_dma_start(
                out=xg_sb[xgi][:, :], out_offset=None,
                in_=ag_xn.ap()[:, :],
                in_offset=IndirectOffsetOnAxis(dgi_sb[:, cglob:cglob + 1], 0)),
                inc=sig, inck=16,
                waits=((s_cc2, 1), (s_vc, V_DGI),
                       (s_vc, ig_prev.get(cglob - 2, 0)), (s_gp, s_gp.n),
                       (s_tc, s_tc.n)))
            IG = (sig, sig.n)
            for kc in range(KC):
                pt = [PC, PD][kc % 2]
                E_("tensor", lambda e, xgi=xgi, kc=kc, pt=pt: e.transpose(
                    pt[:, 0:128].bitcast(BF16)[:, 0:128],
                    xg_sb[xgi][:, kc * 128:(kc + 1) * 128], eyeB[:, :]),
                    inc=s_tc, waits=(IG, (s_vc, s_vc.n), (s_cl, CL)))
                T_NOW = s_tc.n
                E_("vector", lambda e, kc=kc, c2=c2, pt=pt: e.tensor_copy(
                    out=dispE[:, kc, c2 * 128:(c2 + 1) * 128],
                    in_=pt[:, 0:128].bitcast(BF16)[:, 0:128]),
                    inc=s_vc, waits=((s_tc, T_NOW),))
            ig_prev[cglob] = s_vc.n
        V_DISP = s_vc.n
        for g in range(NG):
            gidx = e_i * NG + g
            wb = gidx % 2
            for tg in range(NTG):
                y2ps = [PBIG[:, 0:D], PBIG[:, 1024:1024 + D]]
                for fcl in range(G):
                    hb = fcl % 2
                    hps = [PC, PD][hb]
                    for kc in range(KC):
                        E_("tensor", lambda e, kc=kc, wb=wb, fcl=fcl, hps=hps,
                           tg=tg: e.matmul(
                               hps[:, 0:TGW], w1g[wb][:, fcl, kc, :],
                               dispE[:, kc, tg * TGW:(tg + 1) * TGW],
                               start=(kc == 0), stop=(kc == KC - 1)),
                            inc=s_tc,
                            waits=(w1v[gidx], (s_vc, V_DISP),
                                   (s_sc, s_sc.n)))
                    T_H = s_tc.n
                    E_("scalar", lambda e, hb=hb, hps=hps: e.activation(
                        out=h_sbs[hb][:, :], in_=hps[:, 0:TGW], func=AF.Relu),
                        inc=s_sc, waits=((s_tc, T_H),))
                    SC_H = s_sc.n
                    for tc2 in range(NT2):
                        for ocb in range(D // OW):
                            E_("tensor", lambda e, fcl=fcl, tc2=tc2, ocb=ocb,
                               wb=wb, hb=hb, y2ps=y2ps: e.matmul(
                                   y2ps[tc2][:, ocb * OW:(ocb + 1) * OW],
                                   h_sbs[hb][:, tc2 * 128:(tc2 + 1) * 128],
                                   w2g[wb][:, fcl, ocb * OW:(ocb + 1) * OW],
                                   start=(fcl == 0), stop=(fcl == G - 1)),
                                inc=s_tc,
                                waits=((s_sc, SC_H), (s_vc, s_vc.n) if fcl == 0
                                       else (s_sc, SC_H)))
                T_GRP = s_tc.n
                for tc2 in range(NT2):
                    row = tg * NT2 + tc2
                    if g == 0:
                        E_("vector", lambda e, tc2=tc2, row=row, y2ps=y2ps:
                           e.tensor_copy(out=y2acc[:, row, :], in_=y2ps[tc2][:, 0:D]),
                           inc=s_vc, waits=((s_tc, T_GRP),))
                    else:
                        E_("vector", lambda e, tc2=tc2, row=row, y2ps=y2ps:
                           e.tensor_tensor(out=y2acc[:, row, :], in0=y2ps[tc2][:, 0:D],
                                           in1=y2acc[:, row, :], op=ALU.add),
                           inc=s_vc, waits=((s_tc, T_GRP),))
            grp_t[gidx] = s_tc.n
            if gidx + 2 < EPC * NG:
                emit_wload(gidx + 2)
        V_Y2A = s_vc.n
        for row in range(C // 128):
            ybi = row % 2
            sy = s_y2p[ybi]
            E_("vector", lambda e, row=row, ybi=ybi: e.tensor_copy(
                out=y2e_sb[ybi][:, :], in_=y2acc[:, row, :]),
                inc=s_vc, waits=((s_vc, V_Y2A), (sy, sy.n)))
            V_Y2 = s_vc.n
            slot0 = e_i * C + row * 128
            E_("sync", lambda e, slot0=slot0, ybi=ybi: e.dma_start(
                out=y2_dram.ap()[slot0:slot0 + 128, :], in_=y2e_sb[ybi][:, :]),
                inc=sy, inck=16, waits=((s_vc, V_Y2),))
    Y2_ALL = [(s, s.n) for s in s_y2p]

    if "y2" in debug:
        d = dbg_out("y2", (NS, D), BF16)
        E_("sync", lambda e, d=d: e.dma_start(out=d[:, :], in_=y2_dram.ap()[:, :]),
            inc=s_outp[0], inck=16, waits=tuple(Y2_ALL))
    if "tall" in debug:
        d = dbg_out("tall", (H, TROW), BF16)
        E_("sync", lambda e, d=d: e.dma_start(out=d[:, :], in_=tall_dram.ap()[:, :]),
            inc=s_outp[0], inck=16, waits=((s_td, TD_ALL),))
    E_("gpsimd", lambda e: e.collective_compute(
        "AllGather", ALU.bypass, replica_groups=[list(range(NC))],
        ins=[y2_dram.ap().opt()], outs=[ag_y2.ap().opt()]),
        inc=s_cc3, waits=tuple(Y2_ALL))

    regC.reset()
    wo_sb = regC.bf(OC, D)
    E_("gpsimd", lambda e: e.dma_start(out=wo_sb[:, :, :], in_=wo_in[:, :, :]),
        inc=s_wl, inck=16, waits=((s_tc, s_tc.n), (s_vc, s_vc.n)))
    WL_O = (s_wl, s_wl.n)
    regA.reset()
    xr_sb = [regA.f32(D) for _ in range(2)]
    outs_sb = [regA.f32(D) for _ in range(2)]
    y2o_sb = [regA.bf(D) for _ in range(2)]
    out_done_v = {-2: s_vc.n, -1: s_vc.n}
    outst = {}
    T_PRE7 = s_tc.n
    for tc in range(TC):
        bi = tc % 2
        prjps = [PBIG[:, 0:D], PBIG[:, 1024:1024 + D]][bi]
        sio = s_iop2[bi]
        E_("gpsimd", lambda e, tc=tc, bi=bi: e.indirect_dma_start(
            out=y2o_sb[bi][:, :], out_offset=None,
            in_=ag_y2.ap()[:, :],
            in_offset=IndirectOffsetOnAxis(ridx_sb[:, tc:tc + 1], 0)),
            inc=sio, inck=16,
            waits=((s_cc3, 1), (s_vc, V_RIDX), (s_vc, out_done_v.get(tc - 2, 0)),
                   (s_tc, T_PRE7)))
        IO = (sio, sio.n)
        E_("sync", lambda e, tc=tc, bi=bi: e.dma_start(
            out=xr_sb[bi][:, :], in_=x_in[tc * 128:(tc + 1) * 128, :]),
            inc=s_xl, inck=16, waits=((s_vc, out_done_v.get(tc - 2, 0)),
                                      (s_tc, T_PRE7)))
        XR = s_xl.n
        OW = min(512, D)
        for oc in range(OC):
            for ocb in range(D // OW):
                E_("tensor", lambda e, oc=oc, ocb=ocb, tc=tc, OW=OW, prjps=prjps: e.matmul(
                    prjps[:, ocb * OW:(ocb + 1) * OW],
                    attn_fm[:, oc, tc * 128:(tc + 1) * 128],
                    wo_sb[:, oc, ocb * OW:(ocb + 1) * OW],
                    start=(oc == 0), stop=(oc == OC - 1)),
                    inc=s_tc,
                    waits=((s_vc, V_ATTN), WL_O,
                           (s_vc, out_done_v.get(tc - 2, 0))))
        T_PRJ = s_tc.n
        E_("vector", lambda e, bi=bi, prjps=prjps: e.tensor_tensor(
            out=outs_sb[bi][:, :], in0=prjps[:, 0:D], in1=xr_sb[bi][:, :], op=ALU.add),
            inc=s_vc, waits=((s_tc, T_PRJ), (s_xl, XR),
                             (s_outp[bi], outst.get(tc - 2, 0))))
        E_("vector", lambda e, bi=bi, tc=tc: e.scalar_tensor_tensor(
            out=outs_sb[bi][:, :], in0=y2o_sb[bi][:, :], scalar=wown_sb[:, tc:tc + 1],
            in1=outs_sb[bi][:, :], op0=ALU.mult, op1=ALU.add),
            inc=s_vc, waits=((s_vc, s_vc.n), IO, (s_vc, V_WOWN)))
        V_OUT = s_vc.n
        out_done_v[tc] = V_OUT
        so = s_outp[bi]
        E_("sync", lambda e, tc=tc, bi=bi: e.dma_start(
            out=out_t[tc * 128:(tc + 1) * 128, :], in_=outs_sb[bi][:, :]),
            inc=so, inck=16, waits=((s_vc, V_OUT),))
        outst[tc] = so.n
    def _final(e, tot=[(s.h, s.n) for s in s_outp]):
        for h, v in tot:
            e.wait_ge(h, v)
    rec.ops["sync"].append(_final)

    with nc.Block() as block:
        rec.replay(block)
    es.close()
    nc.finalize()
    return nc, dbg


def host_inputs(cfg: Cfg, inputs):
    S, D, H, dh, F_, NC, E = (cfg.S, cfg.D, cfg.H, cfg.dh, cfg.F, cfg.NC, cfg.E)
    KC, FC, OC, TC, EPC = cfg.KC, cfg.FC, cfg.OC, cfg.TC, cfg.EPC
    x = np.ascontiguousarray(inputs["x"], dtype=np.float32)
    qkv_w = np.asarray(inputs["qkv_w"], dtype=np.float32)
    out_w = np.asarray(inputs["out_w"], dtype=np.float32)
    bias_p = np.asarray(inputs["bias_p"], dtype=np.float32).reshape(H, 1)
    bias_a = np.asarray(inputs["bias_a"], dtype=np.float32).reshape(H, 1)
    freqs = np.asarray(inputs["rope_freqs"], dtype=np.float32).reshape(1, dh // 2)
    Wg = np.asarray(inputs["Wg"], dtype=np.float32)
    w1 = np.asarray(inputs["w1"], dtype=np.float32)
    w2 = np.asarray(inputs["w2"], dtype=np.float32)
    HD = H * dh

    def tile_lhsT(wT):
        t = wT.reshape(KC, 128, OC, 128)
        return np.ascontiguousarray(t.transpose(1, 2, 0, 3))

    wq_t = tile_lhsT(qkv_w[0:HD].T)
    wk_t = tile_lhsT(qkv_w[HD:2 * HD].T)
    wv_t = np.ascontiguousarray(qkv_w[2 * HD:3 * HD].T.reshape(KC, 128, HD)
                                .transpose(1, 0, 2))
    wo_t = np.ascontiguousarray(out_w.T.reshape(OC, 128, D).transpose(1, 0, 2))
    wg_t = np.ascontiguousarray(Wg.reshape(KC, 128, E).transpose(1, 0, 2))
    bp16 = np.zeros((16, 1), np.float32); bp16[:H] = bias_p
    ba16 = np.zeros((16, 1), np.float32); ba16[:H] = bias_a
    in_maps = []
    for c in range(NC):
        m = {
            "x": x[c], "wq": wq_t, "wk": wk_t, "wv": wv_t, "wo": wo_t,
            "wg": wg_t, "bp": bp16, "ba": ba16, "fr": freqs,
            "twoc": np.full((128, 1), 2.0 * c, np.float32),
        }
        so = np.zeros((128, TC), np.float32)
        for jj in range(TC):
            so[c * TC + jj, jj] = 1.0
        m["sown"] = so
        for e_i in range(EPC):
            ww1 = w1[c * EPC + e_i].reshape(KC, 128, FC, 128)
            m[f"w1_{e_i}"] = np.ascontiguousarray(ww1.transpose(1, 2, 0, 3))
            ww2 = w2[c * EPC + e_i].reshape(FC, 128, D)
            m[f"w2_{e_i}"] = np.ascontiguousarray(ww2.transpose(1, 0, 2))
        in_maps.append(m)
    return in_maps



def kernel(L0_attn_x, L0_attn_qkv_w, L0_attn_out_w, L0_attn_bias_p,
           L0_attn_bias_a, L0_attn_rope_freqs, L0_ffn_Wg, L0_ffn_w1, L0_ffn_w2):
    from concourse.bass_utils import run_bass_kernel_spmd
    cfg = FULL
    inputs = {
        "x": np.asarray(L0_attn_x), "qkv_w": np.asarray(L0_attn_qkv_w),
        "out_w": np.asarray(L0_attn_out_w), "bias_p": np.asarray(L0_attn_bias_p),
        "bias_a": np.asarray(L0_attn_bias_a),
        "rope_freqs": np.asarray(L0_attn_rope_freqs),
        "Wg": np.asarray(L0_ffn_Wg), "w1": np.asarray(L0_ffn_w1),
        "w2": np.asarray(L0_ffn_w2),
    }
    in_maps = host_inputs(cfg, inputs)
    nc, _ = build(cfg)
    res = run_bass_kernel_spmd(nc, in_maps, core_ids=list(range(cfg.NC)))
    return np.stack([r["out"] for r in res.results]).astype(np.float32)
